# version 02 (final)
# baseline (speedup 1.0000x reference)
import math
import numpy as np
import ml_dtypes
import concourse.bass as bass
import concourse.mybir as mybir
from concourse.bass_utils import run_bass_kernel_spmd
from contextlib import ExitStack

F32 = mybir.dt.float32
BF16 = mybir.dt.bfloat16
I32 = mybir.dt.int32
U8 = mybir.dt.uint8
AF = mybir.ActivationFunctionType
ALU = mybir.AluOpType
AX = mybir.AxisListType

T = 2048
D = 1024
NT = T // 128
ND = D // 128
NQ = 4
LN_EPS = 1e-5
V_DIM = 2048
I_DIM = 3072
N_EXP = 8

_QW_LIMIT = None
_F0_CUT = 5
_F0_TILES = 16
_F0_NO_ROUTER = False
_F0_NO_STAGE = False
_F0_NO_EVAC = False

CS_LOG = math.log(0.125 * 1.001)
CONSTS = [-math.log(8.0), 1.0 / 256.0, LN_EPS, -0.5, 1.0 / 1024.0, -1.0,
          0.125, 1.0 / 32767.0, 1e-12, 65536.0, 2.0 ** -16, 1.001, 0.5,
          1.0 / 2048.0, 1.0 / 7.0, 2.0, CS_LOG]

RND_C = 12582912.0

(PH_SETUP, PH_QW, PH_A0, PH_A1, PH_A2, PH_A3, PH_A4,
 PH_F0, PH_F1, PH_F2, PH_F3, PH_L1A, PH_L1B, PH_L1F, PH_L1G,
 PH_OUT) = range(16)


def _host_constants():
    c = {}
    c["ident_f32"] = np.eye(128, dtype=np.float32)
    c["ident_bf"] = np.eye(128, dtype=ml_dtypes.bfloat16)
    c["ones_bf"] = np.ones((128, 128), dtype=ml_dtypes.bfloat16)
    c["ones_f32"] = np.ones((1, 128), dtype=np.float32)
    c["tri01"] = np.triu(np.ones((128, 128), dtype=np.float32))
    dim_half = 512
    base_scale = ((np.arange(0, 1024, 2, dtype=np.float64) + 0.4 * 1024)
                  / (1.4 * 1024))
    pos = np.arange(T, dtype=np.float64)
    scale = base_scale[None, :] ** (pos[:, None] / 512.0)
    inv_freq = 1.0 / 10000.0 ** (np.arange(dim_half, dtype=np.float64) / dim_half)
    sin = np.sin(pos[:, None] * inv_freq[None, :])
    cos = np.cos(pos[:, None] * inv_freq[None, :])
    for tag, sc in (("q", scale), ("k", 1.0 / scale)):
        cos_s = np.repeat(cos * sc, 2, axis=1)
        sin_s = np.repeat(sin * sc, 2, axis=1)
        sin_p = sin_s.reshape(T, 512, 2)[:, :, ::-1].reshape(T, 1024)
        c[f"C{tag}"] = np.ascontiguousarray(cos_s.T).astype(ml_dtypes.bfloat16)
        c[f"S{tag}"] = np.ascontiguousarray(sin_p.T).astype(ml_dtypes.bfloat16)
    R = np.zeros((128, 128), dtype=np.float64)
    for i in range(0, 128, 2):
        R[i, i + 1] = -1.0
        R[i + 1, i] = 1.0
    c["rotT_bf"] = np.ascontiguousarray(R.T).astype(ml_dtypes.bfloat16)
    bo = np.zeros((1024, 16), dtype=ml_dtypes.bfloat16)
    for r in range(1024):
        bo[r, r // 64] = 1.0
    c["blockones16"] = np.ascontiguousarray(
        bo.reshape(8, 128, 16).transpose(1, 0, 2).reshape(1024, 16))
    c["pow2row"] = np.broadcast_to(2.0 ** -np.arange(8, dtype=np.float32),
                                   (128, 8)).copy()
    c["iota8"] = np.broadcast_to(np.arange(8, dtype=np.float32), (128, 8)).copy()
    c["constpool"] = np.broadcast_to(
        np.array(CONSTS, dtype=np.float32), (128, len(CONSTS))).copy()
    return c


class Ctr:
    __slots__ = ("h", "n")

    def __init__(self, h):
        self.h = h
        self.n = 0

    def inc(self, instr, k=1):
        instr.then_inc(self.h, k)
        self.n += k
        return self.n


class Arena:
    def __init__(self, nbytes):
        self.nbytes = nbytes
        self.regions = []

    def alloc(self, name, size, live):
        p0, p1 = live
        size = (size + 63) & ~63
        cands = sorted({0} | {(o + s + 63) & ~63 for o, s, *_ in self.regions})
        for off in cands:
            if off + size > self.nbytes:
                continue
            ok = True
            for o, s, q0, q1, _ in self.regions:
                if not (p1 < q0 or q1 < p0):
                    if not (off + size <= o or o + s <= off):
                        ok = False
                        break
            if ok:
                self.regions.append((off, size, p0, p1, name))
                return off
        raise MemoryError(f"arena: cannot place {name} ({size}B, live {live})")


class Kb:
    pass


def _carve(c, name, shape, dtype, live):
    esz = {F32: 4, BF16: 2, I32: 4, U8: 1}[dtype]
    n = int(np.prod(shape)) * esz
    off = c.arena_alloc.alloc(name, n, live)
    ap = c.arena[:, off:off + int(np.prod(shape)) * esz].bitcast(dtype)
    if len(shape) == 2:
        ap = ap.rearrange("p (a b) -> p a b", a=int(shape[0]))
    elif len(shape) == 3:
        ap = ap.rearrange("p (a b c) -> p a b c", a=int(shape[0]), b=int(shape[1]))
    return ap



def _EV(c, v):
    v.wait_ge(c.VE.h, c.VE.n)
    return c.VE.inc


def _ES(c, s):
    s.wait_ge(c.SE.h, c.SE.n)
    return c.SE.inc


def build_program(upto="END", tap=None, n_cores=8):
    nc = bass.Bass("TRN2", target_bir_lowering=False)
    c = Kb()
    c.nc = nc
    c.upto = upto
    c.tap = tap
    c.n_cores = n_cores
    c.tap_done = False

    inp = {}

    def din(name, shape, dtype=F32):
        inp[name] = nc.dram_tensor("in_" + name, list(shape), dtype, kind="ExternalInput").ap()
        return inp[name]

    din("x0", (T, D))
    din("W_qkv", (D, 4096))
    din("W_g", (D, 2048))
    din("W_proj", (2048, D))
    din("decay", (1, 8))
    din("sw1", (2048, 1024))
    din("sw2", (1024, 2048))
    din("ew1", (8, 2048, 1024))
    din("ew2", (8, 1024, 2048))
    din("router_w", (8, 1024))
    din("ln_gamma", (1, D))
    din("ln_beta", (1, D))
    for w in ("wq", "wk", "wv", "wo"):
        din(w, (D, D))
    for b in ("bq", "bk", "bv", "bo"):
        din(b, (1, D))
    din("w_gate", (I_DIM, D))
    din("w_up", (I_DIM, D))
    din("w_down", (D, I_DIM))
    for k, v in _host_constants().items():
        dt = {np.dtype(np.float32): F32,
              np.dtype(ml_dtypes.bfloat16): BF16}[v.dtype]
        din(k, v.shape, dt)
    c.inp = inp

    c.out = nc.dram_tensor("out", [T, D], F32, kind="ExternalOutput").ap()

    scr = {}

    def dscr(name, shape, dtype):
        scr[name] = nc.dram_tensor(name, list(shape), dtype, kind="Internal").ap()
        return scr[name]

    dscr("scr_gate", (2048, T), BF16)
    dscr("scr_pin", (2048, T), BF16)
    dscr("scr_x1", (T, D), F32)
    dscr("scr_x2", (T, D), F32)
    dscr("scr_x3", (T, D), F32)
    dscr("scr_sw1q", (2048, 1024), BF16)
    dscr("scr_sw2q", (1024, 2048), BF16)
    dscr("scr_ew1q", (8, 2048, 1024), BF16)
    dscr("scr_ew2q", (8, 1024, 2048), BF16)
    dscr("scr_wgq", (I_DIM, D), BF16)
    dscr("scr_wupq", (I_DIM, D), BF16)
    dscr("scr_wdq", (D, I_DIM), BF16)
    dscr("scr_up", (I_DIM, T), BF16)
    dscr("scr_g2", (I_DIM, T), BF16)
    dscr("scr_prod", (I_DIM, T), BF16)
    dscr("cc_in", (1, 128), F32)
    dscr("cc_out", (1, 128), F32)
    c.scr = scr

    with ExitStack() as ctx:
        c.ctx = ctx

        def sb(name, shape, dtype):
            return ctx.enter_context(nc.sbuf_tensor(name, list(shape), dtype))
        c.sb = sb

        c.pb = [ctx.enter_context(nc.psum_tensor(f"pb{i}", [128, 512], F32))
                for i in range(8)]

        def sem(name):
            return Ctr(ctx.enter_context(nc.semaphore(name)))
        c.LD = sem("LD")
        c.GQ = sem("GQ")
        c.ds = [sem(f"DS{i}") for i in range(6)]
        c.gs = [sem(f"GS{i}") for i in range(2)]
        c.TE = sem("TE")
        c.VE = sem("VE")
        c.SE = sem("SE")
        c.GE = sem("GE")
        c.PH = sem("PH")
        c.ST = sem("ST")

        c.block = ctx.enter_context(nc.Block())
        c.phase_idx = 0

        c.ident_f32 = sb("ident_f32", (128, 128), F32)
        c.ident_bf = sb("ident_bf", (128, 128), BF16)
        c.ones_bf = sb("ones_bf", (128, 128), BF16)
        c.ones_f32 = sb("ones_f32", (1, 128), F32)
        c.tri01 = sb("tri01", (128, 128), F32)
        c.rotT = sb("rotT", (128, 128), BF16)
        c.blockones16 = sb("blockones16", (128, 8, 16), BF16)
        c.pow2row = sb("pow2row", (128, 8), F32)
        c.iota8 = sb("iota8", (128, 8), F32)
        c.dmy = sb("dmy_sb", (128, 8), F32)
        c.constpool = sb("constpool", (128, len(CONSTS)), F32)

        def cc(val, p=128):
            idx = CONSTS.index(val)
            return c.constpool[0:p, idx:idx + 1]
        c.cc = cc
        c.logd_rep = sb("logd_rep", (128, 8), F32)
        c.tilecol = sb("tilecol", (128, 8, 16), F32)
        c.dec_row = sb("dec_row", (1, 8), F32)
        c.logd_row = sb("logd_row", (1, 8), F32)

        c.arena_alloc = Arena(nc.sbuf_bytes_remaining - 2048)
        c.arena = sb("arena", (128, c.arena_alloc.nbytes), U8)
        c.ctab = _carve(c, "ctab", (8, 512), BF16, (PH_SETUP, PH_A3))
        c.iota512 = _carve(c, "iota512", (512,), F32, (PH_SETUP, PH_OUT))
        c.iotatc = _carve(c, "iotatc", (16,), F32, (PH_SETUP, PH_OUT))

        _setup(c)
        phases = [("QW", _phase_QW), ("A0", _phase_A0), ("A1", _phase_A1),
                  ("A2", _phase_A2), ("A3", _phase_A3), ("A4", _phase_A4),
                  ("F0", _phase_F0), ("F1", _phase_F1), ("F2", _phase_F2),
                  ("F3", _phase_F3), ("L1A", _phase_L1A),
                  ("L1B", _phase_L1B), ("L1F", _phase_L1F),
                  ("L1G", _phase_L1G)]
        for name, fn in phases:
            fn(c)
            if upto == name:
                break
        if upto in ("F0", "F1", "F2") and tap is None:
            c.phase_idx_live = PH_F0
            _final_from_scr(c, "scr_x1")
        _finish(c)
    return nc, c


BARRIER_STEP = 20


def _barrier(c):
    c.phase_idx += 1
    want = BARRIER_STEP * c.phase_idx

    def fsync(eng):
        eng.wait_ge(c.LD.h, c.LD.n)
        for d in c.ds:
            eng.wait_ge(d.h, d.n)
        c.PH.inc(eng.dma_start(out=c.dmy[0:1, 0:1],
                               in_=c.inp["ones_f32"][0:1, 0:1]), 16)
        eng.wait_ge(c.PH.h, want)

    def ftensor(eng):
        c.PH.inc(eng.matmul(c.pb[7][0:1, 0:1], c.ones_f32[0:1, 0:1],
                            c.ones_f32[0:1, 0:1], start=True, stop=True))
        eng.wait_ge(c.PH.h, want)

    def fvector(eng):
        c.PH.inc(eng.memzero(c.dmy[0:1, 1:2]))
        eng.wait_ge(c.PH.h, want)

    def fscalar(eng):
        c.PH.inc(eng.memzero(c.dmy[0:1, 2:3]))
        eng.wait_ge(c.PH.h, want)

    def fgpsimd(eng):
        eng.wait_ge(c.GQ.h, c.GQ.n)
        for d in c.gs:
            eng.wait_ge(d.h, d.n)
        eng.wait_ge(c.ST.h, c.ST.n)
        c.PH.inc(eng.memset(c.dmy[0:1, 3:4], 0.0))
        eng.wait_ge(c.PH.h, want)

    c.block.sync(fsync)
    c.block.tensor(ftensor)
    c.block.vector(fvector)
    c.block.scalar(fscalar)
    c.block.gpsimd(fgpsimd)
    assert c.PH.n == want


def _tap_fm(c, name, buf):
    if c.tap != name or c.tap_done:
        return
    c.tap_done = True
    shape = [128, int(np.prod(buf.shape[1:]))]
    c.tapd = c.nc.dram_tensor("tapd", shape, buf.dtype, kind="ExternalOutput").ap()
    ST = c.ST
    want = BARRIER_STEP * (c.phase_idx + 1) - 1

    def f(g):
        g.wait_ge(c.PH.h, want)
        src = buf.rearrange("p a b -> p (a b)") if len(buf.shape) == 3 else buf[:]
        ST.inc(g.dma_start(out=c.tapd, in_=src), 16)
    c.block.gpsimd(f)


def _setup(c):
    blk, LD, SE, VE, TE, GE = c.block, c.LD, c.SE, c.VE, c.TE, c.GE

    @blk.sync
    def _(s):
        for dst, src in (
            (c.ident_f32, "ident_f32"), (c.ident_bf, "ident_bf"),
            (c.ones_bf, "ones_bf"), (c.ones_f32, "ones_f32"),
            (c.tri01, "tri01"), (c.rotT, "rotT_bf"),
            (c.blockones16, "blockones16"),
            (c.pow2row, "pow2row"), (c.iota8, "iota8"),
            (c.constpool, "constpool"),
        ):
            LD.inc(s.dma_start(out=dst[:], in_=c.inp[src]), 16)
        LD.inc(s.dma_start(out=c.dec_row[:], in_=c.inp["decay"]), 16)
    ld_done = LD.n

    @blk.gpsimd
    def _(g):
        GE.inc(g.iota(c.iota512[:], pattern=[[1, 512]], base=0,
                      channel_multiplier=0, allow_small_or_imprecise_dtypes=True))
        GE.inc(g.iota(c.iotatc[:], pattern=[[128, 16]], base=-384,
                      channel_multiplier=-1,
                      allow_small_or_imprecise_dtypes=True))

    @blk.scalar
    def _(s):
        s.wait_ge(LD.h, ld_done)
        _ES(c, s)(s.activation(out=c.logd_row[:], in_=c.dec_row[:], func=AF.Ln))
    logd_done = SE.n

    @blk.tensor
    def _(t):
        t.wait_ge(SE.h, logd_done)
        t.wait_ge(LD.h, ld_done)
        TE.inc(t.matmul(c.pb[0][:, 0:8], c.ones_f32[:], c.logd_row[:],
                        start=True, stop=True))
    rep_done = TE.n

    @blk.vector
    def _(v):
        v.wait_ge(TE.h, rep_done)
        _EV(c, v)(v.tensor_copy(out=c.logd_rep[:], in_=c.pb[0][:, 0:8]))
    rep2 = VE.n

    @blk.scalar
    def _(s):
        s.wait_ge(VE.h, rep2)
        s.wait_ge(GE.h, 2)
        for h in range(8):
            _ES(c, s)(s.activation(out=c.ctab[:, h, :], in_=c.iota512[:],
                                func=AF.Exp, scale=c.logd_rep[:, h:h + 1],
                                bias=c.cc(-math.log(8.0))))
            _ES(c, s)(s.activation(out=c.tilecol[:, h, :], in_=c.iotatc[:],
                                func=AF.Exp, scale=c.logd_rep[:, h:h + 1]))
    _barrier(c)




QW_SPECS = [
    ("scr_sw1q", "sw1", 2048, 1024, "bitq", 1),
    ("scr_sw2q", "sw2", 1024, 2048, "bitq", 1),
    ("scr_ew1q", "ew1", 2048, 1024, "bitq", 8),
    ("scr_ew2q", "ew2", 1024, 2048, "bitq", 8),
    ("scr_wgq", "w_gate", 3072, 1024, "lpbq", 1),
    ("scr_wupq", "w_up", 3072, 1024, "lpbq", 1),
    ("scr_wdq", "w_down", 1024, 3072, "lpbq", 1),
]


def _phase_QW(c):
    blk, TE, VE, SE = c.block, c.TE, c.VE, c.SE
    wt = [_carve(c, f"qw_w{i}", (3072,), F32, (PH_QW, PH_QW)) for i in range(2)]
    t1 = _carve(c, "qw_t1", (3072,), F32, (PH_QW, PH_QW))
    vi = _carve(c, "qw_vi", (3072,), I32, (PH_QW, PH_QW))
    t2 = _carve(c, "qw_t2", (3072,), F32, (PH_QW, PH_QW))
    ob = [_carve(c, f"qw_o{i}", (3072,), BF16, (PH_QW, PH_QW)) for i in range(2)]
    scol = _carve(c, "qw_s", (4,), F32, (PH_QW, PH_QW))
    sseg = _carve(c, "qw_sg", (2, 96), F32, (PH_QW, PH_QW))

    tiles = []
    for scr_name, in_name, rows, cols, kind, n_mats in QW_SPECS:
        for m in range(n_mats):
            for r0 in range(0, rows, 128):
                tiles.append((scr_name, in_name, m, r0, cols, kind, n_mats))
    if _QW_LIMIT is not None:
        tiles = tiles[:_QW_LIMIT]

    ld_at = []
    ve_at, se_at = [], []
    ve0, se0 = VE.n, SE.n
    dsb = [c.ds[2].n, c.ds[3].n]

    ve_run, se_run = ve0, se0
    for (_, _, _, _, cols, kind, _) in tiles:
        if kind == "bitq":
            ve_run += 5
            se_run += 3
        else:
            ve_run += 7
        ve_at.append(ve_run)
        se_at.append(se_run)

    @blk.sync
    def _(s):
        for n, (scr_name, in_name, m, r0, cols, kind, n_mats) in enumerate(tiles):
            if n >= 2:
                s.wait_ge(VE.h, ve_at[n - 2])
                s.wait_ge(SE.h, se_at[n - 2])
            src = c.inp[in_name]
            if n_mats > 1:
                src = src[m]
            ld_at.append(c.ds[n % 2].inc(s.dma_start(
                out=wt[n % 2][:, 0:cols], in_=src[r0:r0 + 128, :]), 16))
            if n >= 2:
                sp, ip, mp, rp, colsp, kp, nmp = tiles[n - 2]
                s.wait_ge(VE.h, ve_at[n - 2])
                s.wait_ge(SE.h, se_at[n - 2])
                dst = c.scr[sp]
                if nmp > 1:
                    dst = dst[mp]
                c.ds[2 + n % 2].inc(s.dma_start(
                    out=dst[rp:rp + 128, :], in_=ob[n % 2][:, 0:colsp]), 16)
        for n in (len(tiles) - 2, len(tiles) - 1):
            sp, ip, mp, rp, colsp, kp, nmp = tiles[n]
            s.wait_ge(VE.h, ve_at[n])
            s.wait_ge(SE.h, se_at[n])
            dst = c.scr[sp]
            if nmp > 1:
                dst = dst[mp]
            c.ds[2 + n % 2].inc(s.dma_start(
                out=dst[rp:rp + 128, :], in_=ob[n % 2][:, 0:colsp]), 16)

    @blk.vector
    def _(v):
        for n, (scr_name, in_name, m, r0, cols, kind, n_mats) in enumerate(tiles):
            w = wt[n % 2][:, 0:cols]
            v.wait_ge(c.ds[n % 2].h, ld_at[n])
            if n >= 1:
                v.wait_ge(SE.h, se_at[n - 1])
            if n >= 2:
                v.wait_ge(c.ds[2 + n % 2].h, dsb[n % 2] + 16 * ((n - 2) // 2 + 1))
            if kind == "bitq":
                v.wait_ge(SE.h, se_at[n] - 2)
                _EV(c, v)(v.reduce_sum(out=scol[:, 0:1], in_=t1[:, 0:cols],
                                       axis=AX.X))
                _EV(c, v)(v.tensor_scalar(out=scol[:, 1:2], in0=scol[:, 0:1],
                                          scalar1=1.0 / cols, scalar2=1e-5,
                                          op0=ALU.mult, op1=ALU.max))
                _EV(c, v)(v.reciprocal(out=scol[:, 2:3], in_=scol[:, 1:2]))
                v.wait_ge(SE.h, se_at[n] - 1)
                _EV(c, v)(v.tensor_scalar(out=vi[:, 0:cols].bitcast(F32),
                                          in0=t1[:, 0:cols],
                                          scalar1=RND_C, scalar2=-RND_C,
                                          op0=ALU.add, op1=ALU.add))
                _EV(c, v)(v.tensor_scalar(out=t2[:, 0:cols],
                                          in0=vi[:, 0:cols].bitcast(F32),
                                          scalar1=1.0, scalar2=-1.0,
                                          op0=ALU.min, op1=ALU.max))
            else:
                ng = cols // 32
                wg3 = w.rearrange("p (g e) -> p g e", g=ng)
                v.wait_ge(SE.h, se_at[n])
                _EV(c, v)(v.reduce_max(out=sseg[:, 0, 0:ng], in_=wg3, axis=AX.X,
                                       apply_absolute_value=True))
                _EV(c, v)(v.tensor_scalar(out=sseg[:, 0, 0:ng], in0=sseg[:, 0, 0:ng],
                                          scalar1=1.0 / 7.0, scalar2=1e-12,
                                          op0=ALU.mult, op1=ALU.max))
                _EV(c, v)(v.reciprocal(out=sseg[:, 1, 0:ng], in_=sseg[:, 0, 0:ng]))
                _EV(c, v)(v.tensor_tensor(
                    out=t1[:, 0:cols].rearrange("p (g e) -> p g e", g=ng),
                    in0=wg3,
                    in1=sseg[:, 1, 0:ng][:, :, None].broadcast_to([128, ng, 32]),
                    op=ALU.mult))
                _EV(c, v)(v.tensor_scalar(out=vi[:, 0:cols].bitcast(F32),
                                          in0=t1[:, 0:cols],
                                          scalar1=RND_C, scalar2=-RND_C,
                                          op0=ALU.add, op1=ALU.add))
                _EV(c, v)(v.tensor_scalar(out=t2[:, 0:cols],
                                          in0=vi[:, 0:cols].bitcast(F32),
                                          scalar1=7.0, scalar2=-8.0,
                                          op0=ALU.min, op1=ALU.max))
                _EV(c, v)(v.tensor_tensor(
                    out=ob[n % 2][:, 0:cols].rearrange("p (g e) -> p g e", g=ng),
                    in0=t2[:, 0:cols].rearrange("p (g e) -> p g e", g=ng),
                    in1=sseg[:, 0, 0:ng][:, :, None].broadcast_to([128, ng, 32]),
                    op=ALU.mult))

    if c.tap == "QWdbg":
        c.tapd = c.nc.dram_tensor("tapd", [128, 4 * 3072], F32,
                                  kind="ExternalOutput").ap()
        c.tap_done = True
        want = BARRIER_STEP * (c.phase_idx + 1) - 1

        def ftap(g):
            g.wait_ge(c.PH.h, want)
            c.ST.inc(g.dma_start(out=c.tapd[:, 0:3072], in_=t1[:]), 16)
            c.ST.inc(g.dma_start(out=c.tapd[:, 3072:6144],
                                 in_=vi[:].bitcast(F32)), 16)
            c.ST.inc(g.dma_start(out=c.tapd[:, 6144:9216], in_=t2[:]), 16)
            c.ST.inc(g.dma_start(out=c.tapd[:, 9216:10752],
                                 in_=ob[0].bitcast(F32)[:, 0:1536]), 16)
            g.wait_ge(c.ST.h, c.ST.n)
        blk.gpsimd(ftap)

    if c.tap in ("QWs1", "QWw2"):
        qs1 = _carve(c, "qs1", (NT, 1024), BF16, (PH_QW, PH_QW))
        c.tapd = c.nc.dram_tensor("tapd", [128, NT * 1024], BF16,
                                  kind="ExternalOutput").ap()
        c.tap_done = True
        want = BARRIER_STEP * (c.phase_idx + 1) - 1
        scr_t = c.scr["scr_sw1q"] if c.tap == "QWs1" else c.scr["scr_wgq"][0:2048, :]

        def ftap(g):
            g.wait_ge(c.PH.h, want)
            c.ST.inc(g.dma_start(
                out=qs1[:],
                in_=scr_t.rearrange("(a p) d -> p a d", p=128)), 16)
            g.wait_ge(c.ST.h, c.ST.n)
            c.ST.inc(g.dma_start(out=c.tapd,
                                 in_=qs1.rearrange("p a b -> p (a b)")), 16)
            g.wait_ge(c.ST.h, c.ST.n)
        blk.gpsimd(ftap)

    @blk.scalar
    def _(s):
        for n, (scr_name, in_name, m, r0, cols, kind, n_mats) in enumerate(tiles):
            if kind != "bitq":
                continue
            s.wait_ge(c.ds[n % 2].h, ld_at[n])
            if n >= 1:
                s.wait_ge(VE.h, ve_at[n - 1])
            _ES(c, s)(s.activation(out=t1[:, 0:cols], in_=wt[n % 2][:, 0:cols],
                                   func=AF.Abs))
            s.wait_ge(VE.h, ve_at[n] - 2)
            _ES(c, s)(s.mul(out=t1[:, 0:cols], in_=wt[n % 2][:, 0:cols],
                            mul=scol[:, 2:3]))
            s.wait_ge(VE.h, ve_at[n])
            _ES(c, s)(s.mul(out=ob[n % 2][:, 0:cols], in_=t2[:, 0:cols],
                            mul=scol[:, 1:2]))
    _barrier(c)

def _phase_A0(c):
    blk, LD, TE, SE = c.block, c.LD, c.TE, c.SE
    c.xT = _carve(c, "xT", (ND, T), BF16, (PH_A0, PH_A2))
    a0in = [_carve(c, f"a0in{i}", (D,), F32, (PH_A0, PH_A0)) for i in range(2)]
    te0, se0 = TE.n, SE.n
    ld_at = []

    @blk.sync
    def _(s):
        for i in range(NT):
            if i >= 2:
                s.wait_ge(TE.h, te0 + (i - 1) * ND)
            ld_at.append(c.ds[i % 2].inc(
                s.dma_start(out=a0in[i % 2][:],
                            in_=c.inp["x0"][i * 128:(i + 1) * 128, :]), 16))

    @blk.tensor
    def _(t):
        for i in range(NT):
            t.wait_ge(c.ds[i % 2].h, ld_at[i])
            if i >= 2:
                t.wait_ge(SE.h, se0 + 2 * (i - 1))
            for cc in range(ND):
                bank = c.pb[(i % 2) * 2 + cc // 4]
                TE.inc(t.transpose(bank[:, (cc % 4) * 128:(cc % 4) * 128 + 128],
                                   a0in[i % 2][:, cc * 128:cc * 128 + 128],
                                   c.ident_f32[:]))

    @blk.scalar
    def _(s):
        for i in range(NT):
            s.wait_ge(TE.h, te0 + ND * (i + 1))
            for half in range(2):
                bank = c.pb[(i % 2) * 2 + half]
                _ES(c, s)(s.copy(out=c.xT[:, half * 4:half * 4 + 4, i * 128:(i + 1) * 128],
                              in_=bank.rearrange("p (a b) -> p a b", a=4)))
    _tap_fm(c, "A0", c.xT)
    _barrier(c)


def _phase_A1(c):
    blk, LD, GQ, TE, VE, SE = c.block, c.LD, c.GQ, c.TE, c.VE, c.SE
    c.QTl0 = _carve(c, "QTl0", (ND, T), BF16, (PH_A1, PH_A3))
    c.KTl0 = _carve(c, "KTl0", (ND, T), BF16, (PH_A1, PH_A3))
    c.Vtm0 = _carve(c, "Vtm0", (NT, V_DIM), BF16, (PH_A1, PH_A3))
    wbuf = _carve(c, "a1w", (ND, 1024), BF16, (PH_A1, PH_A1))
    tbuf = [_carve(c, f"a1t{i}", (2, 512), BF16, (PH_A1, PH_A1)) for i in range(2)]
    zbuf = [_carve(c, f"a1z{i}", (2, 512), BF16, (PH_A1, PH_A1)) for i in range(2)]

    gq0, te0, ve0, se0 = GQ.n, TE.n, VE.n, SE.n
    tb_at = []
    NI = 64
    te_pa = lambda i: te0 + 10 * i + 8
    te_rot = lambda i: te0 + 10 * i + 10
    te_p1 = te0 + 10 * NI
    te_v = lambda j: te_p1 + 8 * (j + 1)
    part_te = [te0 + 320, te0 + 640, te0 + 896, te0 + 1152]

    @blk.gpsimd
    def _(g):
        for part in range(4):
            if part > 0:
                g.wait_ge(TE.h, part_te[part - 1])
            for ck in range(ND):
                GQ.inc(g.dma_start(
                    out=wbuf[:, ck, :],
                    in_=c.inp["W_qkv"][ck * 128:(ck + 1) * 128,
                                       part * 1024:(part + 1) * 1024]), 16)

    @blk.sync
    def _(s):
        for i in range(NI):
            tag = "q" if i < 32 else "k"
            sl, tb = (i % 32) // 4, i % 4
            if i >= 2:
                s.wait_ge(VE.h, ve0 + 3 * (i - 2) + 2)
            c.ds[i % 2].inc(s.dma_start(
                out=tbuf[i % 2][:, 0, :],
                in_=c.inp[f"C{tag}"][sl * 128:sl * 128 + 128, tb * 512:(tb + 1) * 512]), 16)
            tb_at.append(c.ds[i % 2].inc(s.dma_start(
                out=tbuf[i % 2][:, 1, :],
                in_=c.inp[f"S{tag}"][sl * 128:sl * 128 + 128, tb * 512:(tb + 1) * 512]), 16))

    @blk.tensor
    def _(t):
        for i in range(NI):
            ti = 0 if i < 32 else 1
            sl, tb = (i % 32) // 4, i % 4
            pa, pbk = c.pb[i % 2], c.pb[2 + i % 2]
            if i % 32 == 0:
                t.wait_ge(GQ.h, gq0 + 16 * ND * (ti + 1))
            if i >= 2:
                t.wait_ge(VE.h, ve0 + 3 * (i - 2) + 3)
            col = sl * 128
            for ck in range(ND):
                TE.inc(t.matmul(pa[:], wbuf[:, ck, col:col + 128],
                                c.xT[:, ck, tb * 512:(tb + 1) * 512],
                                start=(ck == 0), stop=(ck == 7)))
            t.wait_ge(VE.h, ve0 + 3 * i + 2)
            TE.inc(t.matmul(pbk[:], c.ident_bf[:], zbuf[i % 2][:, 0, :],
                            start=True, stop=False))
            TE.inc(t.matmul(pbk[:], c.rotT[:], zbuf[i % 2][:, 1, :],
                            start=False, stop=True))

    @blk.vector
    def _(v):
        for i in range(NI):
            tag = "q" if i < 32 else "k"
            sl, tb = (i % 32) // 4, i % 4
            pa, pbk = c.pb[i % 2], c.pb[2 + i % 2]
            v.wait_ge(TE.h, te_pa(i))
            v.wait_ge(c.ds[i % 2].h, tb_at[i])
            _EV(c, v)(v.tensor_tensor(out=zbuf[i % 2][:, 0, :], in0=pa[:],
                                   in1=tbuf[i % 2][:, 0, :], op=ALU.mult))
            _EV(c, v)(v.tensor_tensor(out=zbuf[i % 2][:, 1, :], in0=pa[:],
                                   in1=tbuf[i % 2][:, 1, :], op=ALU.mult))
            v.wait_ge(TE.h, te_rot(i))
            dst = (c.QTl0 if tag == "q" else c.KTl0)[:, sl, tb * 512:(tb + 1) * 512]
            if tag == "q":
                _EV(c, v)(v.tensor_tensor(out=dst, in0=pbk[:],
                                       in1=c.ctab[:, sl, :], op=ALU.mult))
            else:
                _EV(c, v)(v.tensor_copy(out=dst, in_=pbk[:]))

    @blk.tensor
    def _(t):
        for j in range(NT * NQ):
            vh, tt, v2 = j // 32, (j % 32) // 2, j % 2
            bank = c.pb[4 + j % 2]
            if j % 32 == 0:
                t.wait_ge(GQ.h, gq0 + 16 * ND * (3 + vh))
            if j >= 2:
                t.wait_ge(SE.h, se0 + (j - 1))
            for ck in range(ND):
                TE.inc(t.matmul(bank[:], c.xT[:, ck, tt * 128:(tt + 1) * 128],
                                wbuf[:, ck, v2 * 512:(v2 + 1) * 512],
                                start=(ck == 0), stop=(ck == 7)))

    @blk.scalar
    def _(s):
        for j in range(NT * NQ):
            vh, tt, v2 = j // 32, (j % 32) // 2, j % 2
            vb = vh * 2 + v2
            bank = c.pb[4 + j % 2]
            s.wait_ge(TE.h, te_v(j))
            _ES(c, s)(s.copy(out=c.Vtm0[:, tt, vb * 512:(vb + 1) * 512], in_=bank[:]))

    _tap_fm(c, "A1q", c.QTl0)
    _tap_fm(c, "A1k", c.KTl0)
    _tap_fm(c, "A1v", c.Vtm0)
    _barrier(c)


def _phase_A2(c):
    blk, LD, GQ, TE, SE = c.block, c.LD, c.GQ, c.TE, c.SE
    wgs = [_carve(c, f"a2wg{i}", (ND, 128), BF16, (PH_A2, PH_A2)) for i in range(2)]
    stg = [_carve(c, f"a2s{i}", (512,), BF16, (PH_A2, PH_A2)) for i in range(2)]
    gq0, te0, se0 = GQ.n, TE.n, SE.n
    ds_base = [c.ds[2].n, c.ds[3].n]
    wg_at = []
    NI = 16 * NQ

    @blk.gpsimd
    def _(g):
        for sl in range(16):
            if sl >= 2:
                g.wait_ge(TE.h, te0 + 8 * NQ * (sl - 1))
            wg_at.append(c.gs[sl % 2].inc(g.dma_start(
                out=wgs[sl % 2][:],
                in_=c.inp["W_g"].rearrange("(a p) g -> p a g", p=128)[
                    :, :, sl * 128:(sl + 1) * 128]), 16))

    @blk.tensor
    def _(t):
        for i in range(NI):
            sl, tb = i // NQ, i % NQ
            bank = c.pb[i % 2]
            if tb == 0:
                t.wait_ge(c.gs[sl % 2].h, wg_at[sl])
            if i >= 2:
                t.wait_ge(SE.h, se0 + (i - 1))
            for ck in range(ND):
                TE.inc(t.matmul(bank[:], wgs[sl % 2][:, ck, :],
                                c.xT[:, ck, tb * 512:(tb + 1) * 512],
                                start=(ck == 0), stop=(ck == 7)))

    @blk.scalar
    def _(s):
        for i in range(NI):
            bank = c.pb[i % 2]
            s.wait_ge(TE.h, te0 + 8 * (i + 1))
            if i >= 2:
                s.wait_ge(c.ds[2 + i % 2].h, ds_base[i % 2] + 16 * ((i - 2) // 2 + 1))
            _ES(c, s)(s.activation(out=stg[i % 2][:], in_=bank[:], func=AF.Silu))

    @blk.sync
    def _(s):
        for i in range(NI):
            sl, tb = i // NQ, i % NQ
            s.wait_ge(SE.h, se0 + (i + 1))
            c.ds[2 + i % 2].inc(s.dma_start(
                out=c.scr["scr_gate"][sl * 128:sl * 128 + 128, tb * 512:(tb + 1) * 512],
                in_=stg[i % 2][:]), 16)
    _barrier(c)


def _phase_A3(c):
    blk, LD, GQ, TE, VE, SE = c.block, c.LD, c.GQ, c.TE, c.VE, c.SE
    SMR = 4
    c.trimask = _carve(c, "trimask", (8, 4, 128), BF16, (PH_A3, PH_A3))

    @blk.vector
    def _(v):
        for h in range(8):
            for r in range(4):
                _EV(c, v)(v.tensor_scalar(out=c.trimask[:, h, r, :], in0=c.tri01[:],
                                         scalar1=c.tilecol[:, h, 3 - r:4 - r],
                                         scalar2=None, op0=ALU.mult))

    sm = _carve(c, "a3sm", (SMR, 512), BF16, (PH_A3, PH_A3))
    avbf = _carve(c, "a3av", (2, 512), BF16, (PH_A3, PH_A3))
    sqbf = _carve(c, "a3sq", (2, 512), BF16, (PH_A3, PH_A3))
    mu = _carve(c, "a3mu", (512,), F32, (PH_A3, PH_A3))
    ex2 = _carve(c, "a3ex2", (512,), F32, (PH_A3, PH_A3))
    var = _carve(c, "a3var", (512,), F32, (PH_A3, PH_A3))
    rstd = _carve(c, "a3rstd", (512,), F32, (PH_A3, PH_A3))
    xc = _carve(c, "a3xc", (2, 512), F32, (PH_A3, PH_A3))
    gsl = [_carve(c, f"a3g{i}", (2, 512), BF16, (PH_A3, PH_A3)) for i in range(2)]
    pst = [_carve(c, f"a3p{i}", (2, 512), BF16, (PH_A3, PH_A3)) for i in range(2)]

    gq0, te0, ve0, se0 = GQ.n, TE.n, VE.n, SE.n
    g_at, p_at = [], []

    iters = [(h, qt) for h in range(8) for qt in range(NQ)]

    def njt(qt):
        return 4 * qt + 4

    te_counts = []
    run = te0
    te_detail = []
    for (h, qt) in iters:
        d = {}
        d["start"] = run
        per = []
        for jt in range(njt(qt)):
            per.append(run + 1)
            run += 3
        d["jt"] = per
        run += 4
        d["end"] = run
        te_detail.append(d)

    se_detail = []
    run_s = se0
    ve_detail = []
    run_v = ve0
    for (h, qt) in iters:
        d = {"evac": []}
        dv = {"evac": []}
        for jt in range(njt(qt)):
            if jt < 4 * qt:
                run_s += 1
                d["evac"].append(("s", run_s))
                dv["evac"].append(None)
            else:
                jr = jt - 4 * qt
                run_v += 1
                if jr < 3:
                    run_s += 1
                    d["evac"].append(("s", run_s))
                else:
                    d["evac"].append(None)
                dv["evac"].append(("v", run_v))
        run_s += 2
        d["av"] = run_s
        run_v += 1
        dv["sq"] = run_v
        run_s += 2
        d["stats"] = run_s
        run_v += 2
        dv["var"] = run_v
        run_s += 2
        d["rstd"] = run_s
        run_v += 6
        dv["norm"] = run_v
        se_detail.append(d)
        ve_detail.append(dv)

    @blk.sync
    def _(s):
        for n, (h, qt) in enumerate(iters):
            if n >= 2:
                s.wait_ge(VE.h, ve_detail[n - 2]["norm"])
            B = qt * 512
            g_at.append(c.ds[n % 2].inc(s.dma_start(
                out=gsl[n % 2].rearrange("p a b -> p (a b)"),
                in_=c.scr["scr_gate"].rearrange("(a p) t -> p a t", p=128)[
                    :, 2 * h:2 * h + 2, B:B + 512]), 16))
            if n >= 1:
                s.wait_ge(VE.h, ve_detail[n - 1]["norm"])
                hp, qp = iters[n - 1]
                p_at.append(c.ds[2 + (n - 1) % 2].inc(s.dma_start(
                    out=c.scr["scr_pin"].rearrange("(a p) t -> p a t", p=128)[
                        :, 2 * hp:2 * hp + 2, qp * 512:qp * 512 + 512],
                    in_=pst[(n - 1) % 2].rearrange("p a b -> p (a b)")), 16))
        s.wait_ge(VE.h, ve_detail[-1]["norm"])
        hp, qp = iters[-1]
        p_at.append(c.ds[2 + (len(iters) - 1) % 2].inc(s.dma_start(
            out=c.scr["scr_pin"].rearrange("(a p) t -> p a t", p=128)[
                :, 2 * hp:2 * hp + 2, qp * 512:qp * 512 + 512],
            in_=pst[(len(iters) - 1) % 2].rearrange("p a b -> p (a b)")), 16))

    @blk.tensor
    def _(t):
        for n, (h, qt) in enumerate(iters):
            B = qt * 512
            for jt in range(njt(qt)):
                c0 = max(0, jt * 128 - B)
                sbank = c.pb[jt % 2]
                ev = _a3_prev_evac(se_detail, ve_detail, n, jt, 2)
                if ev is not None:
                    kind, val = ev
                    if kind == "both":
                        t.wait_ge(SE.h, val[0])
                        t.wait_ge(VE.h, val[1])
                    else:
                        t.wait_ge((SE if kind == "s" else VE).h, val)
                TE.inc(t.matmul(sbank[:, c0:512],
                                c.KTl0[:, h, jt * 128:jt * 128 + 128],
                                c.QTl0[:, h, B + c0:B + 512],
                                start=True, stop=True))
                ev = se_detail[n]["evac"][jt]
                dv = ve_detail[n]["evac"][jt]
                if ev is not None:
                    t.wait_ge(SE.h, ev[1])
                if dv is not None:
                    t.wait_ge(VE.h, dv[1])
                if jt == 0 and n >= 1:
                    t.wait_ge(SE.h, se_detail[n - 1]["av"])
                for vs in range(2):
                    TE.inc(t.matmul(c.pb[2 + vs][:, c0:512],
                                    c.Vtm0[:, jt, h * 256 + vs * 128:h * 256 + (vs + 1) * 128],
                                    sm[:, jt % SMR, c0:512],
                                    start=(jt == 0), stop=(jt == njt(qt) - 1)))
            t.wait_ge(VE.h, ve_detail[n]["sq"])
            for vs in range(2):
                TE.inc(t.matmul(c.pb[4][:], c.ones_bf[:], avbf[:, vs, :],
                                start=(vs == 0), stop=(vs == 1)))
            for vs in range(2):
                TE.inc(t.matmul(c.pb[5][:], c.ones_bf[:], sqbf[:, vs, :],
                                start=(vs == 0), stop=(vs == 1)))

    @blk.scalar
    def _(s):
        for n, (h, qt) in enumerate(iters):
            B = qt * 512
            for jt in range(njt(qt)):
                c0 = max(0, jt * 128 - B)
                sbank = c.pb[jt % 2]
                ev = se_detail[n]["evac"][jt]
                if ev is None:
                    continue
                s.wait_ge(TE.h, te_detail[n]["jt"][jt])
                if jt < 4 * qt:
                    _ES(c, s)(s.activation(out=sm[:, jt % SMR, c0:512], in_=sbank[:, c0:512],
                                        func=AF.Copy,
                                        scale=c.tilecol[:, h, (4 * qt - jt) + 3:(4 * qt - jt) + 4]))
                else:
                    _ES(c, s)(s.activation(out=sm[:, jt % SMR, c0 + 128:512],
                                        in_=sbank[:, c0 + 128:512], func=AF.Copy,
                                        scale=c.tilecol[:, h, (4 * qt - jt) + 3:(4 * qt - jt) + 4]))
            t_end = te_detail[n]["end"] - 4
            s.wait_ge(TE.h, t_end)
            _ES(c, s)(s.copy(out=avbf[:, 0, :], in_=c.pb[2][:]))
            _ES(c, s)(s.copy(out=avbf[:, 1, :], in_=c.pb[3][:]))
            s.wait_ge(TE.h, te_detail[n]["end"])
            _ES(c, s)(s.mul(out=mu[:], in_=c.pb[4][:], mul=c.cc(1.0 / 256.0)))
            _ES(c, s)(s.mul(out=ex2[:], in_=c.pb[5][:], mul=c.cc(1.0 / 256.0)))
            s.wait_ge(VE.h, ve_detail[n]["var"])
            _ES(c, s)(s.activation(out=var[:], in_=var[:], func=AF.Ln, bias=c.cc(LN_EPS)))
            _ES(c, s)(s.activation(out=rstd[:], in_=var[:], func=AF.Exp, scale=c.cc(-0.5)))

    @blk.vector
    def _(v):
        for n, (h, qt) in enumerate(iters):
            B = qt * 512
            for jt in range(njt(qt)):
                dv = ve_detail[n]["evac"][jt]
                if dv is None:
                    continue
                c0 = max(0, jt * 128 - B)
                jr = jt - 4 * qt
                sbank = c.pb[jt % 2]
                v.wait_ge(TE.h, te_detail[n]["jt"][jt])
                _EV(c, v)(v.tensor_tensor(out=sm[:, jt % SMR, c0:c0 + 128],
                                       in0=sbank[:, c0:c0 + 128],
                                       in1=c.trimask[:, h, jr, :], op=ALU.mult))
            v.wait_ge(SE.h, se_detail[n]["av"])
            _EV(c, v)(v.tensor_tensor(out=sqbf.rearrange("p a b -> p (a b)"),
                                   in0=avbf.rearrange("p a b -> p (a b)"),
                                   in1=avbf.rearrange("p a b -> p (a b)"), op=ALU.mult))
            v.wait_ge(SE.h, se_detail[n]["stats"])
            _EV(c, v)(v.tensor_tensor(out=xc[:, 0, :], in0=mu[:], in1=mu[:], op=ALU.mult))
            _EV(c, v)(v.tensor_sub(out=var[:], in0=ex2[:], in1=xc[:, 0, :]))
            v.wait_ge(SE.h, se_detail[n]["rstd"])
            v.wait_ge(c.ds[n % 2].h, g_at[n])
            if n >= 2:
                v.wait_ge(c.ds[2 + n % 2].h, p_at[n - 2])
            for vs in range(2):
                _EV(c, v)(v.tensor_tensor(out=xc[:, vs, :], in0=avbf[:, vs, :],
                                       in1=mu[:], op=ALU.subtract))
                _EV(c, v)(v.tensor_tensor(out=xc[:, vs, :], in0=xc[:, vs, :],
                                       in1=rstd[:], op=ALU.mult))
                _EV(c, v)(v.tensor_tensor(out=pst[n % 2][:, vs, :], in0=xc[:, vs, :],
                                       in1=gsl[n % 2][:, vs, :], op=ALU.mult))
    _barrier(c)


def _a3_gate_loads(n):
    return 2 * n + 1


def _a3_prev_evac(se_detail, ve_detail, n, jt, rot):
    pj = jt - rot
    pn = n
    while pj < 0:
        pn -= 1
        if pn < 0:
            return None
        npj = len(se_detail[pn]["evac"])
        pj += npj
    ev = se_detail[pn]["evac"][pj]
    dv = ve_detail[pn]["evac"][pj]
    if ev is not None and dv is not None:
        return ("both", (ev[1], dv[1]))
    return ev or dv


def _phase_A4(c):
    blk, LD, GQ, TE, VE, SE = c.block, c.LD, c.GQ, c.TE, c.VE, c.SE
    wproj = _carve(c, "a4wp", (16, 1024), BF16, (PH_A4, PH_A4))
    pin = [_carve(c, f"a4pin{i}", (16, 128), BF16, (PH_A4, PH_A4)) for i in range(2)]
    xsl = [_carve(c, f"a4x{i}", (D,), F32, (PH_A4, PH_A4)) for i in range(2)]
    ssl = _carve(c, "a4s", (D,), F32, (PH_A4, PH_A4))
    junk = _carve(c, "a4junk", (D,), F32, (PH_A4, PH_A4))
    x1sl = [_carve(c, f"a4o{i}", (D,), F32, (PH_A4, PH_A4)) for i in range(2)]
    mcol = _carve(c, "a4m", (4,), F32, (PH_A4, PH_A4))

    gq0, te0, ve0, se0 = GQ.n, TE.n, VE.n, SE.n
    pin_at, x_at, st_at = [], [], []

    @blk.gpsimd
    def _(g):
        for ck in range(16):
            GQ.inc(g.dma_start(out=wproj[:, ck, :],
                               in_=c.inp["W_proj"][ck * 128:(ck + 1) * 128, :]), 16)

    @blk.sync
    def _(s):
        for tt in range(NT):
            if tt >= 2:
                s.wait_ge(TE.h, te0 + 32 * (tt - 1))
            pin_at.append(c.ds[tt % 2].inc(s.dma_start(
                out=pin[tt % 2][:],
                in_=c.scr["scr_pin"].rearrange("(a p) t -> p a t", p=128)[
                    :, :, tt * 128:(tt + 1) * 128]), 16))
            if tt >= 2:
                s.wait_ge(VE.h, ve0 + 3 * (tt - 1))
            x_at.append(c.ds[2 + tt % 2].inc(
                s.dma_start(out=xsl[tt % 2][:],
                            in_=c.inp["x0"][tt * 128:(tt + 1) * 128, :]), 16))
            if tt >= 1:
                s.wait_ge(SE.h, se0 + 8 * tt)
                st_at.append(c.ds[4 + (tt - 1) % 2].inc(
                    s.dma_start(out=c.scr["scr_x1"][(tt - 1) * 128:tt * 128, :],
                                in_=x1sl[(tt - 1) % 2][:]), 16))
        s.wait_ge(SE.h, se0 + 8 * NT)
        st_at.append(c.ds[4 + (NT - 1) % 2].inc(
            s.dma_start(out=c.scr["scr_x1"][(NT - 1) * 128:NT * 128, :],
                        in_=x1sl[(NT - 1) % 2][:]), 16))

    @blk.tensor
    def _(t):
        t.wait_ge(GQ.h, gq0 + 256)
        for tt in range(NT):
            t.wait_ge(c.ds[tt % 2].h, pin_at[tt])
            if tt >= 2:
                t.wait_ge(VE.h, ve0 + 3 * (tt - 1))
            for ck in range(16):
                for half in range(2):
                    TE.inc(t.matmul(c.pb[(tt % 2) * 2 + half][:],
                                    pin[tt % 2][:, ck, :],
                                    wproj[:, ck, half * 512:(half + 1) * 512],
                                    start=(ck == 0), stop=(ck == 15)))

    @blk.vector
    def _(v):
        for tt in range(NT):
            v.wait_ge(TE.h, te0 + 32 * (tt + 1))
            v.wait_ge(c.ds[2 + tt % 2].h, x_at[tt])
            if tt >= 1:
                v.wait_ge(SE.h, se0 + 8 * (tt - 1) + 4)
            b0, b1 = c.pb[(tt % 2) * 2], c.pb[(tt % 2) * 2 + 1]
            _EV(c, v)(v.tensor_tensor(out=ssl[:, 0:512], in0=b0[:],
                                   in1=xsl[tt % 2][:, 0:512], op=ALU.add))
            _EV(c, v)(v.tensor_tensor(out=ssl[:, 512:1024], in0=b1[:],
                                   in1=xsl[tt % 2][:, 512:1024], op=ALU.add))
            _EV(c, v)(v.reduce_sum(out=mcol[:, 0:1], in_=ssl[:], axis=AX.X))

    @blk.scalar
    def _(s):
        for tt in range(NT):
            s.wait_ge(VE.h, ve0 + 3 * (tt + 1))
            if tt >= 2:
                s.wait_ge(c.ds[4 + tt % 2].h, st_at[tt - 2])
            _ES(c, s)(s.mul(out=mcol[:, 1:2], in_=mcol[:, 0:1], mul=c.cc(1.0 / 1024.0)))
            negmu = _neg(c, s, mcol, 1)
            _ES(c, s)(s.activation(out=junk[:], in_=ssl[:], func=AF.Identity,
                                   bias=negmu))
            _ES(c, s)(s.activation(out=ssl[:], in_=junk[:], func=AF.Square,
                                accum_out=mcol[:, 2:3]))
            _ES(c, s)(s.mul(out=mcol[:, 2:3], in_=mcol[:, 2:3], mul=c.cc(1.0 / 1024.0)))
            _ES(c, s)(s.activation(out=mcol[:, 2:3], in_=mcol[:, 2:3], func=AF.Ln,
                                bias=c.cc(LN_EPS)))
            _ES(c, s)(s.activation(out=mcol[:, 3:4], in_=mcol[:, 2:3], func=AF.Exp,
                                scale=c.cc(-0.5)))
            _ES(c, s)(s.mul(out=x1sl[tt % 2][:], in_=junk[:], mul=mcol[:, 3:4]))

    if c.upto == "A4":
        c.phase_idx_live = PH_A4
        _final_from_scr(c, "scr_x1")
    _barrier(c)


def _neg(c, s, mcol, idx):
    _ES(c, s)(s.mul(out=mcol[:, idx:idx + 1], in_=mcol[:, idx:idx + 1],
                    mul=c.cc(-1.0)))
    return mcol[:, idx:idx + 1]


def _final_from_scr(c, scr_name):
    blk, LD, ST = c.block, c.LD, c.ST
    want = BARRIER_STEP * (c.phase_idx + 1) - 1
    fin = _carve(c, f"fin_{scr_name}", (D,), F32, (c.phase_idx_live, PH_OUT))

    @blk.gpsimd
    def _(g):
        g.wait_ge(c.PH.h, want)
        for tt in range(NT):
            ST.inc(g.dma_start(out=fin[:], in_=c.scr[scr_name][tt * 128:(tt + 1) * 128, :]), 16)
            g.wait_ge(ST.h, ST.n)
            ST.inc(g.dma_start(out=c.out[tt * 128:(tt + 1) * 128, :], in_=fin[:]), 16)
            g.wait_ge(ST.h, ST.n)


def _finish(c):
    blk, ST = c.block, c.ST
    _barrier(c)

    @blk.gpsimd
    def _(g):
        g.wait_ge(ST.h, ST.n)



def _phase_F0(c):
    blk, LD, TE, VE, SE = c.block, c.LD, c.TE, c.VE, c.SE
    c.hT = _carve(c, "hT", (ND, T), BF16, (PH_F0, PH_F2))
    c.gate_tm = _carve(c, "gate_tm", (NT, 8), F32, (PH_F0, PH_F2))
    htm = _carve(c, "htm", (NT, D), F32, (PH_F0, PH_F0))
    gb = _carve(c, "f0gb", (2, D), F32, (PH_F0, PH_F0))
    x1sl = [_carve(c, f"f0x{i}", (D,), F32, (PH_F0, PH_F0)) for i in range(2)]
    rsb = _carve(c, "f0rsb", (D,), F32, (PH_F0, PH_F0))
    rts = _carve(c, "f0rts", (ND, 8), F32, (PH_F0, PH_F0))
    hfbuf = _carve(c, "f0hf", (ND, 128), F32, (PH_F0, PH_F0))
    lgsb = _carve(c, "f0lg", (T,), F32, (PH_F0, PH_F0))
    lgtm = _carve(c, "f0lgt", (NT, 8), F32, (PH_F0, PH_F0))
    xc = _carve(c, "f0xc", (D,), F32, (PH_F0, PH_F0))
    junk = _carve(c, "f0junk", (D,), F32, (PH_F0, PH_F0))
    mc = _carve(c, "f0mc", (8,), F32, (PH_F0, PH_F0))

    ld0, te0, ve0, se0 = LD.n, TE.n, VE.n, SE.n
    x_at = []
    te_head = te0 + 12
    ve_head = ve0 + 5
    ve_ln = lambda tt: ve_head + 3 * tt
    se_ln = lambda tt: se0 + 8 * tt
    se_ln_end = se0 + 8 * NT
    _tpt = 8 if _F0_NO_ROUTER else 16
    te_tr = lambda tt: te_head + _tpt * tt
    se_tr = lambda tt: se_ln_end + 2 * tt
    ve_tr = lambda tt: ve_head + 3 * NT + 2 * tt
    te_all = te_head + 16 * NT
    se_lg = se_tr(NT) + 4
    te_lg = te_all + NT
    ve_lg = ve_tr(NT) + 1

    @blk.sync
    def _(s):
        LD.inc(s.dma_start(out=rsb[0:8, :], in_=c.inp["router_w"]), 16)
        LD.inc(s.dma_start(out=gb[0:1, 0, :], in_=c.inp["ln_gamma"]), 16)
        LD.inc(s.dma_start(out=gb[0:1, 1, :], in_=c.inp["ln_beta"]), 16)
        if _F0_CUT < 2:
            return
        for tt in range(NT):
            if tt >= 2:
                s.wait_ge(SE.h, se_ln(tt - 1))
            x_at.append(c.ds[tt % 2].inc(s.dma_start(
                out=x1sl[tt % 2][:],
                in_=c.scr["scr_x1"][tt * 128:(tt + 1) * 128, :]), 16))
    ld_head = ld0 + 48

    @blk.tensor
    def _(t):
        t.wait_ge(LD.h, ld_head)
        for i in range(2):
            for hh in range(2):
                TE.inc(t.matmul(c.pb[2 * i + hh][:, 0:512], c.ones_f32[:],
                                gb[0:1, i, hh * 512:hh * 512 + 512],
                                start=True, stop=True))
        for ck in range(ND):
            TE.inc(t.transpose(c.pb[6][:, ck * 8:ck * 8 + 8],
                               rsb[0:8, ck * 128:(ck + 1) * 128],
                               c.ident_f32[0:8, 0:8]))
        assert TE.n == te_head

    @blk.vector
    def _(v):
        v.wait_ge(TE.h, te_head)
        for i in range(2):
            for hh in range(2):
                _EV(c, v)(v.tensor_copy(out=gb[:, i, hh * 512:hh * 512 + 512],
                                        in_=c.pb[2 * i + hh][:, 0:512]))
        _EV(c, v)(v.tensor_copy(out=rts.rearrange("p a b -> p (a b)"),
                                in_=c.pb[6][:, 0:64]))
        assert VE.n == ve_head
        if _F0_CUT < 2:
            return
        for tt in range(NT):
            v.wait_ge(c.ds[tt % 2].h, x_at[tt])
            if tt >= 1:
                v.wait_ge(SE.h, se_ln(tt))
            _EV(c, v)(v.reduce_sum(out=mc[:, 0:1], in_=x1sl[tt % 2][:], axis=AX.X))
            v.wait_ge(SE.h, se_ln(tt) + 8)
            _EV(c, v)(v.tensor_tensor(out=junk[:], in0=junk[:], in1=gb[:, 0, :],
                                      op=ALU.mult))
            _EV(c, v)(v.tensor_tensor(out=htm[:, tt, :], in0=junk[:],
                                      in1=gb[:, 1, :], op=ALU.add))

    @blk.scalar
    def _(s):
        if _F0_CUT < 2:
            return
        for tt in range(NT):
            s.wait_ge(VE.h, ve_ln(tt) + 1)
            _ES(c, s)(s.mul(out=mc[:, 1:2], in_=mc[:, 0:1], mul=c.cc(1.0 / 1024.0)))
            negmu = _neg2(c, s, mc, 1)
            _ES(c, s)(s.activation(out=xc[:], in_=x1sl[tt % 2][:],
                                   func=AF.Identity, bias=negmu))
            _ES(c, s)(s.activation(out=junk[:], in_=xc[:], func=AF.Square,
                                   accum_out=mc[:, 2:3]))
            _ES(c, s)(s.mul(out=mc[:, 2:3], in_=mc[:, 2:3], mul=c.cc(1.0 / 1024.0)))
            _ES(c, s)(s.activation(out=mc[:, 2:3], in_=mc[:, 2:3], func=AF.Ln,
                                   bias=c.cc(LN_EPS)))
            _ES(c, s)(s.activation(out=mc[:, 3:4], in_=mc[:, 2:3], func=AF.Exp,
                                   scale=c.cc(-0.5)))
            _ES(c, s)(s.mul(out=junk[:], in_=xc[:], mul=mc[:, 3:4]))
        assert SE.n == se_ln_end

    @blk.tensor
    def _(t):
        if _F0_CUT < 3:
            return
        for tt in range(min(NT, _F0_TILES)):
            t.wait_ge(VE.h, ve_ln(tt + 1))
            if tt >= 1:
                if not _F0_NO_STAGE:
                    t.wait_ge(SE.h, se_tr(tt))
                if not _F0_NO_EVAC:
                    t.wait_ge(VE.h, ve_tr(tt))
            ba = c.pb[2 * (tt % 2)]
            bb = c.pb[2 * (tt % 2) + 1]
            for ck in range(ND):
                bank = ba if ck < 4 else bb
                TE.inc(t.transpose(bank[:, (ck % 4) * 128:(ck % 4) * 128 + 128],
                                   htm[:, tt, ck * 128:ck * 128 + 128],
                                   c.ident_f32[:]))
            if not _F0_NO_STAGE:
                t.wait_ge(SE.h, se_tr(tt) + 2)
            if _F0_NO_ROUTER:
                continue
            for ck in range(ND):
                TE.inc(t.matmul(
                    c.pb[4 + tt // 4][0:8, (tt % 4) * 128:(tt % 4) * 128 + 128],
                    rts[:, ck, :], hfbuf[:, ck, :],
                    start=(ck == 0), stop=(ck == 7)))
        if _F0_TILES >= NT and not _F0_NO_ROUTER:
            assert TE.n == te_all

    @blk.scalar
    def _(s):
        if _F0_CUT < 3 or _F0_NO_STAGE:
            return
        for tt in range(min(NT, _F0_TILES)):
            s.wait_ge(TE.h, te_tr(tt) + 8)
            ba = c.pb[2 * (tt % 2)]
            bb = c.pb[2 * (tt % 2) + 1]
            _ES(c, s)(s.copy(out=hfbuf[:, 0:4, :].rearrange("p a b -> p (a b)"),
                             in_=ba[:, 0:512]))
            _ES(c, s)(s.copy(out=hfbuf[:, 4:8, :].rearrange("p a b -> p (a b)"),
                             in_=bb[:, 0:512]))
        if _F0_TILES >= NT:
            assert SE.n == se_tr(NT)

    @blk.vector
    def _(v):
        if _F0_CUT < 3 or _F0_NO_EVAC:
            return
        for tt in range(min(NT, _F0_TILES)):
            v.wait_ge(TE.h, te_tr(tt) + 8)
            if not _F0_NO_STAGE:
                v.wait_ge(SE.h, se_tr(tt) + 2)
            for half in range(2):
                bank = c.pb[2 * (tt % 2) + half]
                _EV(c, v)(v.tensor_copy(
                    out=c.hT[:, half * 4:half * 4 + 4, tt * 128:(tt + 1) * 128],
                    in_=bank.rearrange("p (a b) -> p a b", a=4)))
        if _F0_TILES >= NT:
            assert VE.n == ve_tr(NT)

    @blk.scalar
    def _(s):
        if _F0_CUT < 4:
            return
        s.wait_ge(TE.h, te_all)
        for i in range(4):
            _ES(c, s)(s.copy(out=lgsb[0:8, i * 512:(i + 1) * 512],
                             in_=c.pb[4 + i][0:8, :]))
        assert SE.n == se_lg

    @blk.tensor
    def _(t):
        if _F0_CUT < 4:
            return
        t.wait_ge(SE.h, se_lg)
        for j in range(NT):
            TE.inc(t.transpose(c.pb[6][:, 8 * j:8 * j + 8],
                               lgsb[0:8, j * 128:(j + 1) * 128],
                               c.ident_f32[0:8, 0:8]))
        assert TE.n == te_lg

    @blk.vector
    def _(v):
        if _F0_CUT < 4:
            return
        v.wait_ge(TE.h, te_lg)
        _EV(c, v)(v.tensor_copy(out=lgtm.rearrange("p a b -> p (a b)"),
                                in_=c.pb[6][:, 0:128]))
        assert VE.n == ve_lg
        if _F0_CUT < 5:
            return
        for tt in range(NT):
            lg = lgtm[:, tt, :]
            _EV(c, v)(v.reduce_max(out=mc[:, 4:5], in_=lg, axis=AX.X))
            _EV(c, v)(v.reduce_max(out=mc[:, 5:6], in_=lg, axis=AX.X, negate=True))
            _EV(c, v)(v.tensor_scalar(out=junk[:, 0:8], in0=lg, scalar1=mc[:, 4:5],
                                      scalar2=None, op0=ALU.is_ge))
            _EV(c, v)(v.tensor_tensor(out=junk[:, 0:8], in0=junk[:, 0:8],
                                      in1=c.pow2row[:], op=ALU.mult))
            _EV(c, v)(v.reduce_max(out=mc[:, 6:7], in_=junk[:, 0:8], axis=AX.X))
            _EV(c, v)(v.tensor_scalar(out=junk[:, 0:8], in0=junk[:, 0:8],
                                      scalar1=mc[:, 6:7], scalar2=None,
                                      op0=ALU.is_ge))
            v.wait_ge(SE.h, se_lg + (tt + 1))
            _EV(c, v)(v.reduce_sum(out=mc[:, 7:8], in_=junk[:, 8:16], axis=AX.X))
            _EV(c, v)(v.reciprocal(out=mc[:, 7:8], in_=mc[:, 7:8]))
            _EV(c, v)(v.tensor_scalar(out=c.gate_tm[:, tt, :], in0=junk[:, 0:8],
                                      scalar1=mc[:, 7:8], scalar2=None,
                                      op0=ALU.mult))

    @blk.scalar
    def _(s):
        if _F0_CUT < 5:
            return
        for tt in range(NT):
            s.wait_ge(VE.h, ve_lg + 9 * tt + 2)
            _ES(c, s)(s.activation(out=junk[:, 8:16], in_=lgtm[:, tt, :],
                                   func=AF.Exp, bias=mc[:, 5:6]))
    _tap_fm(c, "F0h", c.hT)
    _tap_fm(c, "F0g", c.gate_tm)
    _tap_fm(c, "F0lg", lgtm)
    _barrier(c)


def _neg2(c, s, mc, idx):
    _ES(c, s)(s.mul(out=mc[:, idx:idx + 1], in_=mc[:, idx:idx + 1],
                    mul=c.cc(-1.0)))
    return mc[:, idx:idx + 1]


def _phase_F1(c):
    blk, LD, TE, VE, SE = c.block, c.LD, c.TE, c.VE, c.SE
    c.s1h1 = _carve(c, "s1h1", (NT, 2048), BF16, (PH_F1, PH_F2))
    c.f1tmp = _carve(c, "f1tmp", (512,), F32, (PH_F1, PH_F2))
    c.fwt = _carve(c, "fwt", (ND, 2048), BF16, (PH_F1, PH_F2))
    c.ffnsum = _carve(c, "ffnsum", (NT, D), F32, (PH_F1, PH_F3))
    _ffn_mlp(c, "scr_sw1q", "scr_sw2q", None)
    if c.tap == "F1w":
        qs1 = _carve(c, "qs1f", (2, 1024), BF16, (PH_F1, PH_F1))
        c.tapd = c.nc.dram_tensor("tapd", [128, NT * 1024], BF16,
                                  kind="ExternalOutput").ap()
        c.tap_done = True
        want = BARRIER_STEP * (c.phase_idx + 1) - 1

        def ftap(g):
            g.wait_ge(c.PH.h, want)
            for i in range(8):
                c.ST.inc(g.dma_start(
                    out=qs1[:],
                    in_=c.scr["scr_sw1q"].rearrange("(a p) d -> p a d", p=128)[
                        :, 2 * i:2 * i + 2, :]), 16)
                g.wait_ge(c.ST.h, c.ST.n)
                c.ST.inc(g.dma_start(out=c.tapd[:, i * 2048:(i + 1) * 2048],
                                     in_=qs1.rearrange("p a b -> p (a b)")), 16)
                g.wait_ge(c.ST.h, c.ST.n)
        blk.gpsimd(ftap)
    _tap_fm(c, "F1a", c.s1h1)
    _tap_fm(c, "F1s", c.ffnsum)
    _barrier(c)


def _ffn_mlp(c, w1_scr, w2_scr, expert):
    blk, LD, TE, VE, SE = c.block, c.LD, c.TE, c.VE, c.SE
    te0, ve0, se0 = TE.n, VE.n, SE.n
    w1_at, w2_at = [], []
    fw2 = c.fwt.rearrange("p a b -> p (a b)").rearrange(
        "p (a b) -> p a b", a=16)[:, :, 0:1024]
    te_a = lambda i: te0 + 8 * i
    te_a_end = te0 + 8 * 64
    se_a = lambda i: se0 + i
    te_b = lambda tt: te_a_end + 32 * tt
    te_b_end = te_a_end + 32 * NT
    ve_b = lambda tt: ve0 + (2 if expert is None else 4) * tt

    @blk.sync
    def _(s):
        for ck in range(ND):
            if expert is not None and ck == 0:
                s.wait_ge(TE.h, c.f2_te_prev)
            src = c.scr[w1_scr] if expert is None else c.scr[w1_scr][expert]
            w1_at.append(c.ds[0].inc(s.dma_start_transpose(
                c.fwt[:, ck, :], src[:, ck * 128:(ck + 1) * 128]), 16))
        s.wait_ge(TE.h, te_a_end)
        for ck in range(NT):
            src = c.scr[w2_scr] if expert is None else c.scr[w2_scr][expert]
            w2_at.append(c.ds[0].inc(s.dma_start_transpose(
                fw2[:, ck, :], src[:, ck * 128:(ck + 1) * 128]), 16))

    @blk.tensor
    def _(t):
        t.wait_ge(c.ds[0].h, w1_at[-1])
        t.wait_ge(SE.h, se0)
        t.wait_ge(VE.h, ve0)
        for i in range(64):
            sl, tb = i // NQ, i % NQ
            bank = c.pb[i % 2]
            if i >= 2:
                t.wait_ge(SE.h, se_a(i - 1))
            for ck in range(ND):
                TE.inc(t.matmul(bank[:], c.fwt[:, ck, sl * 128:sl * 128 + 128],
                                c.hT[:, ck, tb * 512:(tb + 1) * 512],
                                start=(ck == 0), stop=(ck == 7)))
        t.wait_ge(c.ds[0].h, w2_at[-1])
        t.wait_ge(SE.h, se_a(64))
        for tt in range(NT):
            if tt >= 2:
                t.wait_ge(VE.h, ve_b(tt - 1))
            for ck in range(NT):
                for hh in range(2):
                    TE.inc(t.matmul(c.pb[2 + (tt % 2) * 2 + hh][:],
                                    c.s1h1[:, ck, tt * 128:(tt + 1) * 128],
                                    fw2[:, ck, hh * 512:(hh + 1) * 512],
                                    start=(ck == 0), stop=(ck == 15)))

    @blk.scalar
    def _(s):
        for i in range(64):
            sl, tb = i // NQ, i % NQ
            bank = c.pb[i % 2]
            s.wait_ge(TE.h, te_a(i + 1))
            _ES(c, s)(s.activation(out=c.s1h1[:, sl, tb * 512:(tb + 1) * 512],
                                   in_=bank[:], func=AF.Silu))

    @blk.vector
    def _(v):
        tmp = getattr(c, "f1tmp", None)
        for tt in range(NT):
            v.wait_ge(TE.h, te_b(tt + 1))
            for hh in range(2):
                bank = c.pb[2 + (tt % 2) * 2 + hh]
                dst = c.ffnsum[:, tt, hh * 512:(hh + 1) * 512]
                if expert is None:
                    _EV(c, v)(v.tensor_copy(out=dst, in_=bank[:]))
                else:
                    _EV(c, v)(v.tensor_scalar(
                        out=tmp[:], in0=bank[:],
                        scalar1=c.gate_tm[:, tt, expert:expert + 1],
                        scalar2=None, op0=ALU.mult))
                    _EV(c, v)(v.tensor_tensor(out=dst, in0=dst, in1=tmp[:],
                                              op=ALU.add))
    c.f2_te_prev = TE.n


def _phase_F2(c):
    for e in range(N_EXP):
        _ffn_mlp(c, "scr_ew1q", "scr_ew2q", e)
    _barrier(c)


def _phase_F3(c):
    blk, LD, TE, VE, SE = c.block, c.LD, c.TE, c.VE, c.SE
    c.x2T = _carve(c, "x2T", (ND, T), BF16, (PH_F3, PH_L1A))
    x1sl = [_carve(c, f"f3x{i}", (D,), F32, (PH_F3, PH_F3)) for i in range(2)]
    ssl = _carve(c, "f3s", (D,), F32, (PH_F3, PH_F3))
    junk = _carve(c, "f3junk", (D,), F32, (PH_F3, PH_F3))
    x2sl = [_carve(c, f"f3o{i}", (D,), F32, (PH_F3, PH_F3)) for i in range(2)]
    mcol = _carve(c, "f3m", (4,), F32, (PH_F3, PH_F3))
    te0, ve0, se0 = TE.n, VE.n, SE.n
    x_at, st_at = [], []
    ve_ln = lambda tt: ve0 + 3 * tt + 2 * max(0, tt - 2)
    se_ln = lambda tt: se0 + 8 * tt
    te_tr = lambda tt: te0 + 8 * tt
    def ve_evac(k):
        if k <= NT - 3:
            return ve0 + 5 * k + 11
        return ve0 + 3 * NT + 2 * (NT - 2) + 2 * (k - (NT - 2) + 1)

    @blk.sync
    def _(s):
        for tt in range(NT):
            if tt >= 2:
                s.wait_ge(SE.h, se_ln(tt - 1))
            x_at.append(c.ds[tt % 2].inc(s.dma_start(
                out=x1sl[tt % 2][:],
                in_=c.scr["scr_x1"][tt * 128:(tt + 1) * 128, :]), 16))
            if tt >= 1:
                s.wait_ge(SE.h, se_ln(tt))
                st_at.append(c.ds[2 + (tt - 1) % 2].inc(s.dma_start(
                    out=c.scr["scr_x2"][(tt - 1) * 128:tt * 128, :],
                    in_=x2sl[(tt - 1) % 2][:]), 16))
        s.wait_ge(SE.h, se_ln(NT))
        st_at.append(c.ds[2 + (NT - 1) % 2].inc(s.dma_start(
            out=c.scr["scr_x2"][(NT - 1) * 128:NT * 128, :],
            in_=x2sl[(NT - 1) % 2][:]), 16))

    def _evac_x2(v, tt):
        v.wait_ge(TE.h, te_tr(tt) + 8)
        for half in range(2):
            bank = c.pb[(tt % 2) * 2 + half]
            _EV(c, v)(v.tensor_copy(
                out=c.x2T[:, half * 4:half * 4 + 4, tt * 128:(tt + 1) * 128],
                in_=bank.rearrange("p (a b) -> p a b", a=4)))

    @blk.vector
    def _(v):
        for tt in range(NT):
            v.wait_ge(c.ds[tt % 2].h, x_at[tt])
            if tt >= 1:
                v.wait_ge(SE.h, se_ln(tt - 1) + 4)
            _EV(c, v)(v.tensor_tensor(out=ssl[:, 0:512], in0=c.ffnsum[:, tt, 0:512],
                                      in1=x1sl[tt % 2][:, 0:512], op=ALU.add))
            _EV(c, v)(v.tensor_tensor(out=ssl[:, 512:1024],
                                      in0=c.ffnsum[:, tt, 512:1024],
                                      in1=x1sl[tt % 2][:, 512:1024], op=ALU.add))
            _EV(c, v)(v.reduce_sum(out=mcol[:, 0:1], in_=ssl[:], axis=AX.X))
            if tt >= 2:
                _evac_x2(v, tt - 2)
        _evac_x2(v, NT - 2)
        _evac_x2(v, NT - 1)

    @blk.scalar
    def _(s):
        for tt in range(NT):
            s.wait_ge(VE.h, ve_ln(tt) + 3)
            if tt >= 2:
                s.wait_ge(c.ds[2 + tt % 2].h, st_at[tt - 2])
                s.wait_ge(TE.h, te_tr(tt - 1))
            _ES(c, s)(s.mul(out=mcol[:, 1:2], in_=mcol[:, 0:1], mul=c.cc(1.0 / 1024.0)))
            negmu = _neg2(c, s, mcol, 1)
            _ES(c, s)(s.activation(out=junk[:], in_=ssl[:], func=AF.Identity,
                                   bias=negmu))
            _ES(c, s)(s.activation(out=ssl[:], in_=junk[:], func=AF.Square,
                                   accum_out=mcol[:, 2:3]))
            _ES(c, s)(s.mul(out=mcol[:, 2:3], in_=mcol[:, 2:3], mul=c.cc(1.0 / 1024.0)))
            _ES(c, s)(s.activation(out=mcol[:, 2:3], in_=mcol[:, 2:3], func=AF.Ln,
                                   bias=c.cc(LN_EPS)))
            _ES(c, s)(s.activation(out=mcol[:, 3:4], in_=mcol[:, 2:3], func=AF.Exp,
                                   scale=c.cc(-0.5)))
            _ES(c, s)(s.mul(out=x2sl[tt % 2][:], in_=junk[:], mul=mcol[:, 3:4]))

    @blk.tensor
    def _(t):
        for tt in range(NT):
            t.wait_ge(SE.h, se_ln(tt) + 8)
            if tt >= 2:
                t.wait_ge(VE.h, ve_evac(tt - 2))
            for cc in range(ND):
                bank = c.pb[(tt % 2) * 2 + cc // 4]
                TE.inc(t.transpose(bank[:, (cc % 4) * 128:(cc % 4) * 128 + 128],
                                   x2sl[tt % 2][:, cc * 128:cc * 128 + 128],
                                   c.ident_f32[:]))

    if c.upto == "F3":
        c.phase_idx_live = PH_F3
        _final_from_scr(c, "scr_x2")
    _barrier(c)


def _phase_L1A(c):
    blk, LD, GQ, TE, VE, SE = c.block, c.LD, c.GQ, c.TE, c.VE, c.SE
    c.QT1 = _carve(c, "QT1", (ND, T), BF16, (PH_L1A, PH_L1A))
    c.KT1 = _carve(c, "KT1", (ND, T), BF16, (PH_L1A, PH_L1A))
    c.Vtm1 = _carve(c, "Vtm1", (NT, 1040), BF16, (PH_L1A, PH_L1A))
    c.attnT = _carve(c, "attnT", (ND, T), BF16, (PH_L1A, PH_L1B))
    wbuf = _carve(c, "l1w", (ND, 1024), BF16, (PH_L1A, PH_L1A))
    bsb = _carve(c, "l1b", (2, 8), F32, (PH_L1A, PH_L1A))
    bvrow = _carve(c, "l1bv", (1024,), BF16, (PH_L1A, PH_L1A))
    sq2 = _carve(c, "l1sq", (2, 512), BF16, (PH_L1A, PH_L1A))
    mcol = _carve(c, "l1mc", (4,), F32, (PH_L1A, PH_L1A))
    mrow = _carve(c, "l1mr", (16,), F32, (PH_L1A, PH_L1A))
    negM = _carve(c, "l1nm", (16,), F32, (PH_L1A, PH_L1A))
    pt = _carve(c, "l1pt", (4, 512), BF16, (PH_L1A, PH_L1A))
    rrow = _carve(c, "l1rr", (2, 512), F32, (PH_L1A, PH_L1A))
    rrep = _carve(c, "l1rp", (512,), F32, (PH_L1A, PH_L1A))
    v4 = c.Vtm1.rearrange("p a (h e) -> p a h e", h=16)

    gq0, te0, ve0, se0 = GQ.n, TE.n, VE.n, SE.n
    te_q = lambda i: te0 + 8 * (i + 1)
    te_k = lambda i: te0 + 256 + 8 * (i + 1)
    te_v = lambda i: te0 + 512 + 9 * (i + 1)
    te_p1 = te0 + 800
    se_q = lambda i: se0 + i + 1
    se_p1 = se0 + 64
    ve_v = lambda i: ve0 + 1 + (i + 1)
    ve_p1 = ve0 + 33

    @blk.gpsimd
    def _(g):
        GQ.inc(g.dma_start(out=bvrow[0:1, :], in_=c.inp["bv"]), 16)
        for pi, w in enumerate(("wq", "wk", "wv")):
            if pi > 0:
                g.wait_ge(TE.h, te0 + 256 * pi)
            for ck in range(ND):
                GQ.inc(g.dma_start(out=wbuf[:, ck, :],
                                   in_=c.inp[w][ck * 128:(ck + 1) * 128, :]), 16)

    @blk.sync
    def _(s):
        LD.inc(s.dma_start(
            out=bsb[:, 0, :],
            in_=c.inp["bq"].rearrange("o (a p) -> (o p) a", p=128),
            allow_slow_non_contiguous=True), 16)
        LD.inc(s.dma_start(
            out=bsb[:, 1, :],
            in_=c.inp["bk"].rearrange("o (a p) -> (o p) a", p=128),
            allow_slow_non_contiguous=True), 16)
    ld_b = LD.n

    @blk.vector
    def _(v):
        _EV(c, v)(v.memset(v4[:, :, :, 64:65], 1.0))
        for i in range(32):
            tt, vb = i // 2, i % 2
            v.wait_ge(TE.h, te_v(i))
            _EV(c, v)(v.tensor_copy(
                out=v4[:, tt, 8 * vb:8 * vb + 8, 0:64],
                in_=c.pb[2 + i % 2].rearrange("p (h e) -> p h e", h=8)))

    @blk.tensor
    def _(t):
        for pi in range(2):
            t.wait_ge(GQ.h, gq0 + 16 + 16 * ND * (pi + 1))
            for i in range(32):
                sl, tb = i // 4, i % 4
                bank = c.pb[i % 2]
                if pi == 0 and i >= 2:
                    t.wait_ge(SE.h, se_q(i - 2))
                elif pi == 1:
                    t.wait_ge(SE.h, se_q(30 + i) if i < 2 else se_q(32 + i - 2))
                for ck in range(ND):
                    TE.inc(t.matmul(bank[:], wbuf[:, ck, sl * 128:sl * 128 + 128],
                                    c.x2T[:, ck, tb * 512:(tb + 1) * 512],
                                    start=(ck == 0), stop=(ck == 7)))
        t.wait_ge(GQ.h, gq0 + 16 + 16 * ND * 3)
        t.wait_ge(VE.h, ve0 + 1)
        for i in range(32):
            tt, vb = i // 2, i % 2
            bank = c.pb[2 + i % 2]
            if i >= 2:
                t.wait_ge(VE.h, ve_v(i - 2))
            for ck in range(ND):
                TE.inc(t.matmul(bank[:], c.x2T[:, ck, tt * 128:(tt + 1) * 128],
                                wbuf[:, ck, vb * 512:(vb + 1) * 512],
                                start=(ck == 0), stop=False))
            TE.inc(t.matmul(bank[:], c.ones_bf[0:1, :],
                            bvrow[0:1, vb * 512:(vb + 1) * 512],
                            start=False, stop=True))

    @blk.scalar
    def _(s):
        s.wait_ge(LD.h, ld_b)
        for pi in range(2):
            dst = c.QT1 if pi == 0 else c.KT1
            for i in range(32):
                sl, tb = i // 4, i % 4
                s.wait_ge(TE.h, te_q(i) if pi == 0 else te_k(i))
                _ES(c, s)(s.activation(
                    out=dst[:, sl, tb * 512:(tb + 1) * 512], in_=c.pb[i % 2][:],
                    func=AF.Identity, bias=bsb[:, pi, sl:sl + 1]))

    te_cs = lambda j: te_p1 + 8 * (j + 1)
    vtt = lambda kk: ve_p1 + 10 * (kk // 8) + (kk % 8) + 1
    ve_red = lambda j: ve_p1 + 10 * j + 9
    ve_cs_end = ve_p1 + 80

    @blk.vector
    def _(v):
        v.wait_ge(SE.h, se_p1)
        kk = 0
        for si, srcb in enumerate((c.QT1, c.KT1)):
            for tb in range(4):
                j = si * 4 + tb
                for ck in range(ND):
                    if kk >= 2:
                        v.wait_ge(TE.h, te_p1 + kk - 1)
                    sl = srcb[:, ck, tb * 512:(tb + 1) * 512]
                    _EV(c, v)(v.tensor_tensor(out=sq2[:, kk % 2, :], in0=sl,
                                              in1=sl, op=ALU.mult))
                    kk += 1
                v.wait_ge(TE.h, te_cs(j))
                _EV(c, v)(v.reduce_max(out=mcol[0:16, 0:1],
                                       in_=c.pb[6][0:16, :], axis=AX.X))
                if tb == 0:
                    _EV(c, v)(v.tensor_copy(out=mrow[0:16, si:si + 1],
                                            in_=mcol[0:16, 0:1]))
                else:
                    _EV(c, v)(v.tensor_max(out=mrow[0:16, si:si + 1],
                                           in0=mrow[0:16, si:si + 1],
                                           in1=mcol[0:16, 0:1]))

    @blk.tensor
    def _(t):
        kk = 0
        for si in range(2):
            for tb in range(4):
                j = si * 4 + tb
                if j >= 1:
                    t.wait_ge(VE.h, ve_red(j - 1))
                for ck in range(ND):
                    t.wait_ge(VE.h, vtt(kk))
                    TE.inc(t.matmul(c.pb[6][0:16, :], c.blockones16[:, ck, :],
                                    sq2[:, kk % 2, :], start=(ck == 0),
                                    stop=(ck == 7)))
                    kk += 1

    @blk.scalar
    def _(s):
        s.wait_ge(VE.h, ve_cs_end)
        _ES(c, s)(s.activation(out=mrow[0:16, 0:1], in_=mrow[0:16, 0:1],
                               func=AF.Ln))
        _ES(c, s)(s.activation(out=mrow[0:16, 1:2], in_=mrow[0:16, 1:2],
                               func=AF.Ln))
    se_cs = SE.n

    @blk.vector
    def _(v):
        v.wait_ge(SE.h, se_cs)
        _EV(c, v)(v.tensor_add(out=mrow[0:16, 2:3], in0=mrow[0:16, 0:1],
                               in1=mrow[0:16, 1:2]))
    ve_m = VE.n

    @blk.scalar
    def _(s):
        s.wait_ge(VE.h, ve_m)
        _ES(c, s)(s.activation(out=mrow[0:16, 3:4], in_=mrow[0:16, 2:3],
                               func=AF.Exp, scale=c.cc(0.5, 16),
                               bias=c.cc(CS_LOG, 16)))
        _ES(c, s)(s.mul(out=mrow[0:16, 3:4], in_=mrow[0:16, 3:4],
                        mul=c.cc(-1.0, 16)))
    se_m = SE.n

    @blk.tensor
    def _(t):
        t.wait_ge(SE.h, se_m)
        TE.inc(t.transpose(c.pb[5][0:1, 0:16], mrow[0:16, 3:4],
                           c.ident_f32[0:16, 0:16]))
    te_m1 = TE.n

    @blk.scalar
    def _(s):
        s.wait_ge(TE.h, te_m1)
        _ES(c, s)(s.copy(out=mrow[0:1, 0:16], in_=c.pb[5][0:1, 0:16]))
    se_m2 = SE.n

    @blk.tensor
    def _(t):
        t.wait_ge(SE.h, se_m2)
        TE.inc(t.matmul(c.pb[5][:, 0:16], c.ones_f32[:], mrow[0:1, 0:16],
                        start=True, stop=True))
    te_m2 = TE.n

    @blk.scalar
    def _(s):
        s.wait_ge(TE.h, te_m2)
        _ES(c, s)(s.copy(out=negM[:, 0:16], in_=c.pb[5][:, 0:16]))
    se_m3 = SE.n

    iters = [(h, qt) for h in range(16) for qt in range(NQ)]
    te_d, se_d, ve_d = [], [], []
    te_run, se_run, ve_run = TE.n, SE.n, VE.n
    for (h, qt) in iters:
        njt = 4 * qt + 4
        d = {"score": [], "av": []}
        ds_ = {"exp": []}
        dv = {"tri": {}}
        for st in range(njt):
            te_run += 1
            d["score"].append(te_run)
            se_run += 1
            ds_["exp"].append(se_run)
            if st >= 4 * qt:
                ve_run += 1
                dv["tri"][st] = ve_run
            te_run += 1
            d["av"].append(te_run)
        te_run += 1
        d["bmm"] = te_run
        se_run += 2
        ds_["r"] = se_run
        ve_run += 2
        dv["out"] = ve_run
        te_d.append(d)
        se_d.append(ds_)
        ve_d.append(dv)

    @blk.tensor
    def _(t):
        t.wait_ge(SE.h, se_m3)
        for n, (h, qt) in enumerate(iters):
            njt = 4 * qt + 4
            B = qt * 512
            hr0 = 64 * (h % 2)
            hc = h // 2
            for st in range(njt):
                c0 = max(0, st * 128 - B)
                sb = c.pb[st % 2]
                pm = _l1_prev(se_d, iters, n, st, 2)
                if pm is not None:
                    t.wait_ge(SE.h, pm)
                TE.inc(t.matmul(sb[:, c0:512],
                                c.KT1[hr0:hr0 + 64, hc, st * 128:st * 128 + 128],
                                c.QT1[hr0:hr0 + 64, hc, B + c0:B + 512],
                                start=True, stop=True))
                t.wait_ge(SE.h, se_d[n]["exp"][st])
                if st in ve_d[n]["tri"]:
                    t.wait_ge(VE.h, ve_d[n]["tri"][st])
                if st == 0 and n >= 1:
                    t.wait_ge(VE.h, ve_d[n - 1]["out"])
                TE.inc(t.matmul(c.pb[2][0:65, c0:512],
                                c.Vtm1[:, st, 65 * h:65 * h + 65],
                                pt[:, st % 4, c0:512],
                                start=(st == 0), stop=(st == njt - 1)))
            t.wait_ge(SE.h, se_d[n]["r"])
            if n >= 1:
                t.wait_ge(VE.h, ve_d[n - 1]["out"] - 1)
            TE.inc(t.matmul(c.pb[3][0:64, :], c.ones_f32[0:1, 0:64],
                            rrow[0:1, 1, :], start=True, stop=True))

    @blk.scalar
    def _(s):
        for n, (h, qt) in enumerate(iters):
            njt = 4 * qt + 4
            B = qt * 512
            for st in range(njt):
                c0 = max(0, st * 128 - B)
                s.wait_ge(TE.h, te_d[n]["score"][st])
                if st >= 4:
                    s.wait_ge(TE.h, te_d[n]["av"][st - 4])
                else:
                    pav = _l1_prev_av(te_d, iters, n, st)
                    if pav is not None:
                        s.wait_ge(TE.h, pav)
                _ES(c, s)(s.activation(out=pt[:, st % 4, c0:512],
                                       in_=c.pb[st % 2][:, c0:512], func=AF.Exp,
                                       scale=c.cc(0.125),
                                       bias=negM[:, h:h + 1]))
            s.wait_ge(TE.h, te_d[n]["av"][njt - 1])
            _ES(c, s)(s.activation(out=rrow[0:1, 0, :], in_=c.pb[2][64:65, :],
                                   func=AF.Ln))
            _ES(c, s)(s.activation(out=rrow[0:1, 1, :], in_=rrow[0:1, 0, :],
                                   func=AF.Exp, scale=c.cc(-1.0, 1)))

    @blk.vector
    def _(v):
        for n, (h, qt) in enumerate(iters):
            njt = 4 * qt + 4
            B = qt * 512
            hr0 = 64 * (h % 2)
            hc = h // 2
            for st in range(4 * qt, njt):
                c0 = st * 128 - B
                v.wait_ge(SE.h, se_d[n]["exp"][st])
                _EV(c, v)(v.tensor_tensor(out=pt[:, st % 4, c0:c0 + 128],
                                          in0=pt[:, st % 4, c0:c0 + 128],
                                          in1=c.tri01[:], op=ALU.mult))
            v.wait_ge(TE.h, te_d[n]["bmm"])
            _EV(c, v)(v.tensor_copy(out=rrep[0:64, :], in_=c.pb[3][0:64, :]))
            _EV(c, v)(v.tensor_tensor(
                out=c.attnT[hr0:hr0 + 64, hc, B:B + 512],
                in0=c.pb[2][0:64, :], in1=rrep[0:64, :], op=ALU.mult))
    _tap_fm(c, "L1q", c.QT1)
    _tap_fm(c, "L1k", c.KT1)
    _tap_fm(c, "L1at", c.attnT)
    if c.upto == "L1A" and c.tap is None:
        c.phase_idx_live = PH_L1A
        _final_from_scr(c, "scr_x2")
    _barrier(c)


def _l1_prev(se_d, iters, n, st, back):
    ps = st - back
    pn = n
    while ps < 0:
        pn -= 1
        if pn < 0:
            return None
        ps += 4 * iters[pn][1] + 4
    return se_d[pn]["exp"][ps]


def _l1_prev_av(te_d, iters, n, st):
    ps = st - 4
    pn = n
    while ps < 0:
        pn -= 1
        if pn < 0:
            return None
        ps += 4 * iters[pn][1] + 4
    return te_d[pn]["av"][ps]


def _phase_L1B(c):
    blk, LD, GQ, TE, VE, SE = c.block, c.LD, c.GQ, c.TE, c.VE, c.SE
    c.x3T = _carve(c, "x3T", (ND, T), F32, (PH_L1B, PH_L1F))
    c.l1f_rs = _carve(c, "l1frs", (2,), F32, (PH_L1B, PH_L1F))
    wo = _carve(c, "l1wo", (ND, 1024), BF16, (PH_L1B, PH_L1B))
    borow = _carve(c, "l1bo", (1024,), BF16, (PH_L1B, PH_L1B))
    x2sl = [_carve(c, f"l1x2{i}", (D,), F32, (PH_L1B, PH_L1B)) for i in range(2)]
    ssl = _carve(c, "l1s", (D,), F32, (PH_L1B, PH_L1B))
    junk = _carve(c, "l1junk", (D,), F32, (PH_L1B, PH_L1B))
    x3sl = [_carve(c, f"l1o{i}", (D,), F32, (PH_L1B, PH_L1B)) for i in range(2)]
    mc = _carve(c, "l1m", (8,), F32, (PH_L1B, PH_L1B))

    gq0, te0, ve0, se0 = GQ.n, TE.n, VE.n, SE.n
    x_at, st_at = [], []

    def te_mm_end(tt):
        return te0 + 18 * (tt + 1) + 8 * tt

    def te_tr_end(k):
        return te0 + 18 * (k + 2) + 8 * (k + 1) if k < NT - 1 else te0 + 26 * NT

    ve_pre = ve0 + 1
    ve_add, ve_tail = [], [None] * NT
    run = ve_pre
    for tt in range(NT):
        run += 3
        ve_add.append(run)
        if tt >= 1:
            run += 4
            ve_tail[tt - 1] = run
    run += 4
    ve_tail[NT - 1] = run
    ve_all = run
    se_t = lambda tt: se0 + 8 * (tt + 1)

    @blk.gpsimd
    def _(g):
        GQ.inc(g.dma_start(out=borow[0:1, :], in_=c.inp["bo"]), 16)
        for ck in range(ND):
            GQ.inc(g.dma_start(out=wo[:, ck, :],
                               in_=c.inp["wo"][ck * 128:(ck + 1) * 128, :]), 16)

    @blk.sync
    def _(s):
        for tt in range(NT):
            if tt >= 2:
                s.wait_ge(VE.h, ve_add[tt - 2])
            x_at.append(c.ds[tt % 2].inc(s.dma_start(
                out=x2sl[tt % 2][:],
                in_=c.scr["scr_x2"][tt * 128:(tt + 1) * 128, :]), 16))
            if tt >= 1:
                s.wait_ge(SE.h, se_t(tt - 1))
                st_at.append(c.ds[2 + (tt - 1) % 2].inc(s.dma_start(
                    out=c.scr["scr_x3"][(tt - 1) * 128:tt * 128, :],
                    in_=x3sl[(tt - 1) % 2][:]), 16))
        s.wait_ge(SE.h, se_t(NT - 1))
        st_at.append(c.ds[2 + (NT - 1) % 2].inc(s.dma_start(
            out=c.scr["scr_x3"][(NT - 1) * 128:NT * 128, :],
            in_=x3sl[(NT - 1) % 2][:]), 16))
    ld_b = LD.n

    @blk.tensor
    def _(t):
        t.wait_ge(GQ.h, gq0 + 16 + 16 * ND)
        for tt in range(NT):
            if tt >= 2:
                t.wait_ge(VE.h, ve_add[tt - 2])
            for ck in range(ND):
                for hh in range(2):
                    TE.inc(t.matmul(c.pb[(tt % 2) * 2 + hh][:],
                                    c.attnT[:, ck, tt * 128:(tt + 1) * 128],
                                    wo[:, ck, hh * 512:(hh + 1) * 512],
                                    start=(ck == 0), stop=False))
            for hh in range(2):
                TE.inc(t.matmul(c.pb[(tt % 2) * 2 + hh][:], c.ones_bf[0:1, :],
                                borow[0:1, hh * 512:(hh + 1) * 512],
                                start=False, stop=True))
            if tt >= 1:
                t.wait_ge(SE.h, se_t(tt - 1))
                if tt >= 3:
                    t.wait_ge(VE.h, ve_tail[tt - 3])
                for cc in range(ND):
                    bank = c.pb[4 + ((tt - 1) % 2) * 2 + cc // 4]
                    TE.inc(t.transpose(
                        bank[:, (cc % 4) * 128:(cc % 4) * 128 + 128],
                        x3sl[(tt - 1) % 2][:, cc * 128:cc * 128 + 128],
                        c.ident_f32[:]))
        t.wait_ge(SE.h, se_t(NT - 1))
        t.wait_ge(VE.h, ve_tail[NT - 3])
        for cc in range(ND):
            bank = c.pb[4 + ((NT - 1) % 2) * 2 + cc // 4]
            TE.inc(t.transpose(bank[:, (cc % 4) * 128:(cc % 4) * 128 + 128],
                               x3sl[(NT - 1) % 2][:, cc * 128:cc * 128 + 128],
                               c.ident_f32[:]))

    def _tail(v, k):
        v.wait_ge(SE.h, se_t(k))
        _EV(c, v)(v.reduce_max(out=mc[:, 5:6], in_=x3sl[k % 2][:], axis=AX.X,
                               apply_absolute_value=True))
        _EV(c, v)(v.tensor_max(out=mc[:, 4:5], in0=mc[:, 4:5], in1=mc[:, 5:6]))
        v.wait_ge(TE.h, te_tr_end(k))
        for half in range(2):
            bank = c.pb[4 + (k % 2) * 2 + half]
            _EV(c, v)(v.tensor_copy(
                out=c.x3T[:, half * 4:half * 4 + 4, k * 128:(k + 1) * 128],
                in_=bank.rearrange("p (a b) -> p a b", a=4)))

    @blk.vector
    def _(v):
        _EV(c, v)(v.memset(mc[:, 4:5], 0.0))
        for tt in range(NT):
            v.wait_ge(TE.h, te_mm_end(tt))
            v.wait_ge(c.ds[tt % 2].h, x_at[tt])
            if tt >= 1:
                v.wait_ge(SE.h, se_t(tt - 1) - 4)
            b0, b1 = c.pb[(tt % 2) * 2], c.pb[(tt % 2) * 2 + 1]
            _EV(c, v)(v.tensor_tensor(out=ssl[:, 0:512], in0=b0[:],
                                      in1=x2sl[tt % 2][:, 0:512], op=ALU.add))
            _EV(c, v)(v.tensor_tensor(out=ssl[:, 512:1024], in0=b1[:],
                                      in1=x2sl[tt % 2][:, 512:1024], op=ALU.add))
            _EV(c, v)(v.reduce_sum(out=mc[:, 0:1], in_=ssl[:], axis=AX.X))
            if tt >= 1:
                _tail(v, tt - 1)
        _tail(v, NT - 1)

    @blk.scalar
    def _(s):
        for tt in range(NT):
            s.wait_ge(VE.h, ve_add[tt])
            if tt >= 2:
                s.wait_ge(c.ds[2 + tt % 2].h, st_at[tt - 2])
                s.wait_ge(TE.h, te_tr_end(tt - 2))
            _ES(c, s)(s.mul(out=mc[:, 1:2], in_=mc[:, 0:1],
                            mul=c.cc(1.0 / 1024.0)))
            negmu = _neg2(c, s, mc, 1)
            _ES(c, s)(s.activation(out=junk[:], in_=ssl[:], func=AF.Identity,
                                   bias=negmu))
            _ES(c, s)(s.activation(out=ssl[:], in_=junk[:], func=AF.Square,
                                   accum_out=mc[:, 2:3]))
            _ES(c, s)(s.mul(out=mc[:, 2:3], in_=mc[:, 2:3],
                            mul=c.cc(1.0 / 1024.0)))
            _ES(c, s)(s.activation(out=mc[:, 2:3], in_=mc[:, 2:3], func=AF.Ln,
                                   bias=c.cc(LN_EPS)))
            _ES(c, s)(s.activation(out=mc[:, 3:4], in_=mc[:, 2:3], func=AF.Exp,
                                   scale=c.cc(-0.5)))
            _ES(c, s)(s.mul(out=x3sl[tt % 2][:], in_=junk[:], mul=mc[:, 3:4]))

    @blk.tensor
    def _(t):
        t.wait_ge(VE.h, ve_all)
        TE.inc(t.transpose(c.pb[3][0:1, 0:128], mc[:, 4:5], c.ident_f32[:]))
    te_gm = TE.n

    @blk.vector
    def _(v):
        v.wait_ge(TE.h, te_gm)
        _EV(c, v)(v.tensor_copy(out=junk[0:1, 0:128], in_=c.pb[3][0:1, 0:128]))
    ve_gm = VE.n

    @blk.gpsimd
    def _(g):
        g.wait_ge(VE.h, ve_gm)
        GQ.inc(g.dma_start(out=c.scr["cc_in"], in_=junk[0:1, 0:128]), 16)
        g.wait_ge(GQ.h, GQ.n)
        c.GE.inc(g.collective_compute(
            "AllReduce", ALU.max, replica_groups=[list(range(c.n_cores))],
            ins=[c.scr["cc_in"].opt()], outs=[c.scr["cc_out"].opt()]))
        g.wait_ge(c.GE.h, c.GE.n)
        GQ.inc(g.dma_start(out=junk[0:1, 0:128], in_=c.scr["cc_out"]), 16)
        g.wait_ge(GQ.h, GQ.n)
    gq_cc = GQ.n

    @blk.vector
    def _(v):
        v.wait_ge(GQ.h, gq_cc)
        _EV(c, v)(v.reduce_max(out=mc[0:1, 6:7], in_=junk[0:1, 0:128],
                               axis=AX.X))
        _EV(c, v)(v.tensor_scalar(out=mc[0:1, 6:7], in0=mc[0:1, 6:7],
                                  scalar1=1.0 / 32767.0, scalar2=1e-12,
                                  op0=ALU.mult, op1=ALU.max))
        _EV(c, v)(v.reciprocal(out=mc[0:1, 7:8], in_=mc[0:1, 6:7]))
    ve_sc = VE.n

    @blk.tensor
    def _(t):
        t.wait_ge(VE.h, ve_sc)
        TE.inc(t.matmul(c.pb[3][:, 0:2], c.ones_f32[:], mc[0:1, 6:8],
                        start=True, stop=True))
    te_sc = TE.n

    @blk.scalar
    def _(s):
        s.wait_ge(TE.h, te_sc)
        _ES(c, s)(s.copy(out=c.l1f_rs[:, 0:2], in_=c.pb[3][:, 0:2]))

    if c.upto == "L1B":
        c.phase_idx_live = PH_L1B
        _final_from_scr(c, "scr_x3")
    _barrier(c)


def _phase_L1F(c):
    blk, LD, GQ, TE, VE, SE = c.block, c.LD, c.GQ, c.TE, c.VE, c.SE
    NSL = I_DIM // 128
    c.xq = _carve(c, "xqT", (ND, T), BF16, (PH_L1F, PH_L1F))
    c.rs2 = _carve(c, "l1rs2", (4,), F32, (PH_L1F, PH_L1G))
    wT = _carve(c, "l1fw", (ND, I_DIM), BF16, (PH_L1F, PH_L1F))
    upsl = [_carve(c, f"l1fu{i}", (T,), BF16, (PH_L1F, PH_L1F)) for i in range(2)]
    t1 = _carve(c, "l1ft1", (T,), F32, (PH_L1F, PH_L1F))
    t2 = _carve(c, "l1ft2", (T,), F32, (PH_L1F, PH_L1F))
    mxf = _carve(c, "l1fmx", (4,), F32, (PH_L1F, PH_L1F))
    mrowf = _carve(c, "l1fmr", (8,), F32, (PH_L1F, PH_L1F))

    te0, ve0, se0 = TE.n, VE.n, SE.n
    ve_rnd = lambda ck: ve0 + 1 + 2 * ck + 1
    ve_clip = lambda ck: ve0 + 1 + 2 * ck + 2
    ve_xq = ve0 + 17
    se_a1 = lambda ck: se0 + 2 * ck + 1
    se_xq = se0 + 16
    te_b_mm = lambda j, tb: te0 + 32 * j + 8 * (tb + 1)
    te_up_end = te0 + 32 * NSL
    se_b = lambda j, tb: se_xq + 4 * j + tb + 1
    ve_b = lambda j: ve_xq + 2 * (j + 1)
    ve_b_end = ve_xq + 4 * NSL

    w_at = {}
    st_b = {}

    @blk.sync
    def _(s):
        for wi, scrw in enumerate(("scr_wupq", "scr_wgq")):
            if wi == 1:
                s.wait_ge(TE.h, te_up_end)
            ats = []
            for ck in range(ND):
                ats.append(c.ds[0].inc(s.dma_start_transpose(
                    wT[:, ck, :], c.scr[scrw][:, ck * 128:(ck + 1) * 128]), 16))
            w_at[wi] = ats
            scrd = "scr_up" if wi == 0 else "scr_g2"
            for sl in range(NSL):
                j = wi * NSL + sl
                s.wait_ge(VE.h, ve_b(j))
                st_b[j] = c.ds[2 + sl % 2].inc(s.dma_start(
                    out=c.scr[scrd][sl * 128:(sl + 1) * 128, :],
                    in_=upsl[sl % 2][:]), 16)

    @blk.vector
    def _(v):
        _EV(c, v)(v.memset(mxf[:, 1:3], 0.0))
        for ck in range(ND):
            v.wait_ge(SE.h, se_a1(ck))
            _EV(c, v)(v.tensor_scalar(out=t2[:], in0=t1[:], scalar1=RND_C,
                                      scalar2=-RND_C, op0=ALU.add, op1=ALU.add))
            _EV(c, v)(v.tensor_scalar(out=t2[:], in0=t2[:], scalar1=32767.0,
                                      scalar2=-32768.0, op0=ALU.min,
                                      op1=ALU.max))
        for wi in range(2):
            for sl in range(NSL):
                j = wi * NSL + sl
                v.wait_ge(SE.h, se_b(j, 3))
                _EV(c, v)(v.reduce_max(out=mxf[:, 0:1], in_=upsl[sl % 2][:],
                                       axis=AX.X, apply_absolute_value=True))
                _EV(c, v)(v.tensor_max(out=mxf[:, 1 + wi:2 + wi],
                                       in0=mxf[:, 1 + wi:2 + wi],
                                       in1=mxf[:, 0:1]))

    @blk.scalar
    def _(s):
        for ck in range(ND):
            if ck >= 1:
                s.wait_ge(VE.h, ve_rnd(ck - 1))
            _ES(c, s)(s.mul(out=t1[:], in_=c.x3T[:, ck, :],
                            mul=c.l1f_rs[:, 1:2]))
            s.wait_ge(VE.h, ve_clip(ck))
            _ES(c, s)(s.mul(out=c.xq[:, ck, :], in_=t2[:],
                            mul=c.l1f_rs[:, 0:1]))
        for wi in range(2):
            for sl in range(NSL):
                j = wi * NSL + sl
                for tb in range(4):
                    s.wait_ge(TE.h, te_b_mm(j, tb))
                    if tb == 0 and j >= 2:
                        s.wait_ge(c.ds[2 + sl % 2].h, st_b[j - 2])
                        s.wait_ge(VE.h, ve_b(j - 2))
                    _ES(c, s)(s.copy(
                        out=upsl[sl % 2][:, tb * 512:(tb + 1) * 512],
                        in_=c.pb[(j % 2) * 2 + tb % 2][:]))

    @blk.tensor
    def _(t):
        for wi in range(2):
            t.wait_ge(c.ds[0].h, w_at[wi][-1])
            if wi == 0:
                t.wait_ge(SE.h, se_xq)
            for sl in range(NSL):
                j = wi * NSL + sl
                for tb in range(4):
                    if tb >= 2:
                        t.wait_ge(SE.h, se_b(j, tb - 2))
                    elif j >= 2:
                        t.wait_ge(SE.h, se_b(j - 2, tb + 2))
                    bank = c.pb[(j % 2) * 2 + tb % 2]
                    for ck in range(ND):
                        TE.inc(t.matmul(
                            bank[:], wT[:, ck, sl * 128:sl * 128 + 128],
                            c.xq[:, ck, tb * 512:(tb + 1) * 512],
                            start=(ck == 0), stop=(ck == 7)))

    @blk.tensor
    def _(t):
        t.wait_ge(VE.h, ve_b_end)
        TE.inc(t.transpose(c.pb[6][0:2, 0:128], mxf[:, 1:3], c.ident_f32[:]))
    te_gm = TE.n

    @blk.vector
    def _(v):
        v.wait_ge(TE.h, te_gm)
        _EV(c, v)(v.tensor_copy(out=t1[0:2, 0:128], in_=c.pb[6][0:2, 0:128]))
        _EV(c, v)(v.reduce_max(out=mrowf[0:2, 0:1], in_=t1[0:2, 0:128],
                               axis=AX.X))
    ve_gm = VE.n

    @blk.tensor
    def _(t):
        t.wait_ge(VE.h, ve_gm)
        TE.inc(t.transpose(c.pb[6][0:1, 0:2], mrowf[0:2, 0:1],
                           c.ident_f32[0:2, 0:2]))
    te_gm2 = TE.n

    @blk.scalar
    def _(s):
        s.wait_ge(TE.h, te_gm2)
        _ES(c, s)(s.copy(out=mrowf[0:1, 2:4], in_=c.pb[6][0:1, 0:2]))
    se_gm = SE.n

    @blk.vector
    def _(v):
        v.wait_ge(SE.h, se_gm)
        _EV(c, v)(v.memset(t1[0:1, 0:128], 0.0))
        _EV(c, v)(v.tensor_copy(out=t1[0:1, 0:2], in_=mrowf[0:1, 2:4]))
    ve_pack = VE.n

    @blk.gpsimd
    def _(g):
        g.wait_ge(VE.h, ve_pack)
        GQ.inc(g.dma_start(out=c.scr["cc_in"], in_=t1[0:1, 0:128]), 16)
        g.wait_ge(GQ.h, GQ.n)
        c.GE.inc(g.collective_compute(
            "AllReduce", ALU.max, replica_groups=[list(range(c.n_cores))],
            ins=[c.scr["cc_in"].opt()], outs=[c.scr["cc_out"].opt()]))
        g.wait_ge(c.GE.h, c.GE.n)
        GQ.inc(g.dma_start(out=mrowf[0:1, 4:6],
                           in_=c.scr["cc_out"][0:1, 0:2]), 16)
        g.wait_ge(GQ.h, GQ.n)
    gq_cc = GQ.n

    @blk.vector
    def _(v):
        v.wait_ge(GQ.h, gq_cc)
        _EV(c, v)(v.tensor_scalar(out=mrowf[0:1, 4:6], in0=mrowf[0:1, 4:6],
                                  scalar1=1.0 / 32767.0, scalar2=1e-12,
                                  op0=ALU.mult, op1=ALU.max))
        _EV(c, v)(v.reciprocal(out=mrowf[0:1, 6:7], in_=mrowf[0:1, 4:5]))
        _EV(c, v)(v.reciprocal(out=mrowf[0:1, 7:8], in_=mrowf[0:1, 5:6]))
    ve_rs = VE.n

    @blk.tensor
    def _(t):
        t.wait_ge(VE.h, ve_rs)
        TE.inc(t.matmul(c.pb[6][:, 0:4], c.ones_f32[:], mrowf[0:1, 4:8],
                        start=True, stop=True))
    te_rs = TE.n

    @blk.scalar
    def _(s):
        s.wait_ge(TE.h, te_rs)
        _ES(c, s)(s.copy(out=c.rs2[:, 0:4], in_=c.pb[6][:, 0:4]))

    if c.upto == "L1F":
        c.phase_idx_live = PH_L1F
        _tap_scr(c, "L1up", "scr_up", I_DIM, T, BF16)
        _tap_scr(c, "L1gate", "scr_g2", I_DIM, T, BF16)
    _barrier(c)


def _phase_L1G(c):
    blk, LD, GQ, TE, VE, SE = c.block, c.LD, c.GQ, c.TE, c.VE, c.SE
    NSL = I_DIM // 128
    wdT = _carve(c, "l1gwd", (NSL, 1024), BF16, (PH_L1G, PH_L1G))
    g2sl = [_carve(c, f"l1gg{i}", (T,), BF16, (PH_L1G, PH_L1G)) for i in range(2)]
    up2 = [_carve(c, f"l1gu{i}", (T,), BF16, (PH_L1G, PH_L1G)) for i in range(2)]
    prodsl = [_carve(c, f"l1gp{i}", (T,), BF16, (PH_L1G, PH_L1G))
              for i in range(2)]
    t1g = _carve(c, "l1gt1", (T,), F32, (PH_L1G, PH_L1G))
    t2g = _carve(c, "l1gt2", (T,), F32, (PH_L1G, PH_L1G))
    t3g = _carve(c, "l1gt3", (T,), F32, (PH_L1G, PH_L1G))
    t4g = _carve(c, "l1gt4", (T,), F32, (PH_L1G, PH_L1G))
    mxg = _carve(c, "l1gmx", (4,), F32, (PH_L1G, PH_L1G))
    rs3 = _carve(c, "l1grs3", (4,), F32, (PH_L1G, PH_L1G))
    mrowg = _carve(c, "l1gmr", (8,), F32, (PH_L1G, PH_L1G))
    prodt = [_carve(c, f"l1gpt{i}", (NSL, 128), BF16, (PH_L1G, PH_L1G))
             for i in range(2)]
    ot = [_carve(c, f"l1got{i}", (NSL, 128), BF16, (PH_L1G, PH_L1G))
          for i in range(2)]
    tq1 = _carve(c, "l1gq1", (I_DIM,), F32, (PH_L1G, PH_L1G))
    tq2 = _carve(c, "l1gq2", (I_DIM,), F32, (PH_L1G, PH_L1G))
    x3sl = [_carve(c, f"l1gx{i}", (D,), F32, (PH_L1G, PH_L1G)) for i in range(2)]
    sslg = _carve(c, "l1gs", (D,), F32, (PH_L1G, PH_L1G))
    junkg = _carve(c, "l1gj", (D,), F32, (PH_L1G, PH_L1G))
    x4sl = [_carve(c, f"l1go{i}", (D,), F32, (PH_L1G, PH_L1G)) for i in range(2)]
    mc2 = _carve(c, "l1gm2", (4,), F32, (PH_L1G, PH_L1G))
    prview = c.scr["scr_prod"].rearrange("(a p) t -> p a t", p=128)

    vv0, ss0, tt0 = VE.n, SE.n, TE.n
    v_ms = vv0 + 1
    vc = lambda sl, k: v_ms + 7 * sl + k
    v_c_end = v_ms + 7 * NSL
    v_ar3a = v_c_end + 1
    v_ar3b = v_ar3a + 4
    vd = lambda sl, k: v_ar3b + 8 * sl + k
    v_d_end = v_ar3b + 8 * NSL
    v_ar4a = v_d_end + 1
    v_ar4b = v_ar4a + 3
    vf = lambda tt, k: v_ar4b + 5 * tt + k
    sc = lambda sl, k: ss0 + 4 * sl + k
    s_rs3 = ss0 + 4 * NSL + 1
    sd = lambda sl, k: s_rs3 + 3 * sl + k
    s_rs4 = s_rs3 + 3 * NSL + 1
    sf = lambda tt, k: s_rs4 + 10 * tt + k
    t_ar3_tr = tt0 + 1
    t_ar3_bc = tt0 + 2
    t_ar4_tr = tt0 + 3
    t_ar4_bc = tt0 + 4
    tf_mm = lambda tt, q: tt0 + 4 + 48 * tt + q

    gq0g = GQ.n
    gq_ar3 = gq0g + 32
    gq_ar4 = gq0g + 64
    c_ld, c_st, d_ld0, d_ld1, d_st, f_ld0, f_ld1, f_st = ({} for _ in range(8))
    wd_at = []

    @blk.sync
    def _(s):
        for sl in range(NSL):
            if sl >= 2:
                s.wait_ge(c.ds[2 + sl % 2].h, c_st[sl - 2])
                s.wait_ge(SE.h, sc(sl - 2, 1))
                s.wait_ge(VE.h, vc(sl - 2, 7))
            c_ld[sl] = c.ds[0].inc(s.dma_start(
                out=g2sl[sl % 2][:],
                in_=c.scr["scr_g2"][sl * 128:(sl + 1) * 128, :]), 16)
            wd_at.append(c.ds[4].inc(s.dma_start_transpose(
                wdT[:, sl, :],
                c.scr["scr_wdq"][:, sl * 128:(sl + 1) * 128]), 16))
            s.wait_ge(VE.h, vc(sl, 5))
            c_st[sl] = c.ds[2 + sl % 2].inc(s.dma_start(
                out=c.scr["scr_g2"][sl * 128:(sl + 1) * 128, :],
                in_=g2sl[sl % 2][:]), 16)
        for sl in range(NSL):
            s.wait_ge(c.ds[2 + sl % 2].h,
                      c_st[sl] if sl >= 2 else c_st[NSL - 2 + sl])
            if sl >= 2:
                s.wait_ge(SE.h, sd(sl - 2, 1))
            d_ld0[sl] = c.ds[0].inc(s.dma_start(
                out=g2sl[sl % 2][:],
                in_=c.scr["scr_g2"][sl * 128:(sl + 1) * 128, :]), 16)
            if sl >= 2:
                s.wait_ge(SE.h, sd(sl - 2, 2))
            d_ld1[sl] = c.ds[1].inc(s.dma_start(
                out=up2[sl % 2][:],
                in_=c.scr["scr_up"][sl * 128:(sl + 1) * 128, :]), 16)
            s.wait_ge(VE.h, vd(sl, 6))
            d_st[sl] = c.ds[2 + sl % 2].inc(s.dma_start(
                out=c.scr["scr_prod"][sl * 128:(sl + 1) * 128, :],
                in_=prodsl[sl % 2][:]), 16)
        s.wait_ge(c.ds[2].h, d_st[NSL - 2])
        s.wait_ge(c.ds[3].h, d_st[NSL - 1])
        for tt in range(NT):
            dsp = c.ds[0] if tt % 2 == 0 else c.ds[5]
            dsx = c.ds[1] if tt % 2 == 0 else c.ds[3]
            if tt >= 2:
                s.wait_ge(SE.h, sf(tt - 2, 1))
            dsp.inc(s.dma_start(
                out=prodt[tt % 2][:, 0:12, :],
                in_=prview[:, 0:12, tt * 128:(tt + 1) * 128]), 16)
            f_ld0[tt] = dsp.inc(s.dma_start(
                out=prodt[tt % 2][:, 12:24, :],
                in_=prview[:, 12:24, tt * 128:(tt + 1) * 128]), 16)
            if tt >= 2:
                s.wait_ge(VE.h, vf(tt - 2, 4))
            f_ld1[tt] = dsx.inc(s.dma_start(
                out=x3sl[tt % 2][:],
                in_=c.scr["scr_x3"][tt * 128:(tt + 1) * 128, :]), 16)
            if tt >= 1:
                s.wait_ge(SE.h, sf(tt - 1, 10))
                f_st[tt - 1] = c.ds[2 + (tt - 1) % 2 * 2].inc(s.dma_start(
                    out=c.out[(tt - 1) * 128:tt * 128, :],
                    in_=x4sl[(tt - 1) % 2][:]), 16)
        s.wait_ge(SE.h, sf(NT - 1, 10))
        f_st[NT - 1] = c.ds[2 + (NT - 1) % 2 * 2].inc(s.dma_start(
            out=c.out[(NT - 1) * 128:NT * 128, :],
            in_=x4sl[(NT - 1) % 2][:]), 16)

    @blk.vector
    def _(v):
        _EV(c, v)(v.memset(mxg[:, 1:3], 0.0))
        for sl in range(NSL):
            v.wait_ge(SE.h, sc(sl, 1))
            _EV(c, v)(v.tensor_scalar(out=t2g[:], in0=t1g[:], scalar1=RND_C,
                                      scalar2=-RND_C, op0=ALU.add, op1=ALU.add))
            _EV(c, v)(v.tensor_scalar(out=t2g[:], in0=t2g[:], scalar1=32767.0,
                                      scalar2=-32768.0, op0=ALU.min,
                                      op1=ALU.max))
            v.wait_ge(SE.h, sc(sl, 4))
            _EV(c, v)(v.tensor_scalar(out=t3g[:], in0=t3g[:], scalar1=RND_C,
                                      scalar2=-RND_C, op0=ALU.add, op1=ALU.add))
            _EV(c, v)(v.tensor_scalar_min(out=t3g[:], in0=t3g[:],
                                          scalar1=65535.0))
            if sl >= 2:
                v.wait_ge(c.ds[2 + sl % 2].h, c_st[sl - 2])
            _EV(c, v)(v.tensor_tensor(out=g2sl[sl % 2][:], in0=t2g[:],
                                      in1=t3g[:], op=ALU.mult))
            _EV(c, v)(v.reduce_max(out=mxg[:, 0:1], in_=g2sl[sl % 2][:],
                                   axis=AX.X, apply_absolute_value=True))
            _EV(c, v)(v.tensor_max(out=mxg[:, 1:2], in0=mxg[:, 1:2],
                                   in1=mxg[:, 0:1]))
        v.wait_ge(TE.h, t_ar3_tr)
        _EV(c, v)(v.tensor_copy(out=t1g[0:1, 0:128], in_=c.pb[6][0:1, 0:128]))
        v.wait_ge(GQ.h, gq_ar3)
        _EV(c, v)(v.reduce_max(out=mrowg[0:1, 1:2], in_=t1g[0:1, 0:128],
                               axis=AX.X))
        _EV(c, v)(v.tensor_scalar(out=mrowg[0:1, 1:2], in0=mrowg[0:1, 1:2],
                                  scalar1=2.0 ** -16 / 32767.0, scalar2=1e-12,
                                  op0=ALU.mult, op1=ALU.max))
        _EV(c, v)(v.reciprocal(out=mrowg[0:1, 2:3], in_=mrowg[0:1, 1:2]))
        _EV(c, v)(v.tensor_scalar_mul(out=mrowg[0:1, 2:3], in0=mrowg[0:1, 2:3],
                                      scalar1=2.0 ** -16))
        for sl in range(NSL):
            v.wait_ge(SE.h, sd(sl, 1))
            _EV(c, v)(v.tensor_scalar(out=t2g[:], in0=t1g[:], scalar1=RND_C,
                                      scalar2=-RND_C, op0=ALU.add, op1=ALU.add))
            _EV(c, v)(v.tensor_scalar(out=t2g[:], in0=t2g[:], scalar1=32767.0,
                                      scalar2=-32768.0, op0=ALU.min,
                                      op1=ALU.max))
            v.wait_ge(SE.h, sd(sl, 2))
            _EV(c, v)(v.tensor_scalar(out=t4g[:], in0=t3g[:], scalar1=RND_C,
                                      scalar2=-RND_C, op0=ALU.add, op1=ALU.add))
            _EV(c, v)(v.tensor_scalar(out=t4g[:], in0=t4g[:], scalar1=32767.0,
                                      scalar2=-32768.0, op0=ALU.min,
                                      op1=ALU.max))
            _EV(c, v)(v.tensor_scalar_mul(out=t4g[:], in0=t4g[:],
                                          scalar1=c.rs2[:, 0:1]))
            v.wait_ge(SE.h, sd(sl, 3))
            if sl >= 2:
                v.wait_ge(c.ds[2 + sl % 2].h, d_st[sl - 2])
            _EV(c, v)(v.tensor_tensor(out=prodsl[sl % 2][:], in0=t1g[:],
                                      in1=t4g[:], op=ALU.mult))
            _EV(c, v)(v.reduce_max(out=mxg[:, 0:1], in_=prodsl[sl % 2][:],
                                   axis=AX.X, apply_absolute_value=True))
            _EV(c, v)(v.tensor_max(out=mxg[:, 2:3], in0=mxg[:, 2:3],
                                   in1=mxg[:, 0:1]))
        v.wait_ge(TE.h, t_ar4_tr)
        _EV(c, v)(v.tensor_copy(out=t1g[0:1, 0:128], in_=c.pb[6][0:1, 0:128]))
        v.wait_ge(GQ.h, gq_ar4)
        _EV(c, v)(v.reduce_max(out=mrowg[0:1, 5:6], in_=t1g[0:1, 0:128],
                               axis=AX.X))
        _EV(c, v)(v.tensor_scalar(out=mrowg[0:1, 5:6], in0=mrowg[0:1, 5:6],
                                  scalar1=1.0 / 32767.0, scalar2=1e-12,
                                  op0=ALU.mult, op1=ALU.max))
        _EV(c, v)(v.reciprocal(out=mrowg[0:1, 6:7], in_=mrowg[0:1, 5:6]))
        for tt in range(NT):
            v.wait_ge(SE.h, sf(tt, 1))
            _EV(c, v)(v.tensor_scalar(out=tq2[:], in0=tq1[:], scalar1=RND_C,
                                      scalar2=-RND_C, op0=ALU.add, op1=ALU.add))
            _EV(c, v)(v.tensor_scalar(out=tq2[:], in0=tq2[:], scalar1=32767.0,
                                      scalar2=-32768.0, op0=ALU.min,
                                      op1=ALU.max))
            v.wait_ge(TE.h, tf_mm(tt, 48))
            v.wait_ge((c.ds[1] if tt % 2 == 0 else c.ds[3]).h, f_ld1[tt])
            if tt >= 1:
                v.wait_ge(SE.h, sf(tt - 1, 6))
            b0, b1 = c.pb[(tt % 2) * 2], c.pb[(tt % 2) * 2 + 1]
            _EV(c, v)(v.tensor_tensor(out=sslg[:, 0:512], in0=b0[:],
                                      in1=x3sl[tt % 2][:, 0:512], op=ALU.add))
            _EV(c, v)(v.tensor_tensor(out=sslg[:, 512:1024], in0=b1[:],
                                      in1=x3sl[tt % 2][:, 512:1024],
                                      op=ALU.add))
            _EV(c, v)(v.reduce_sum(out=mc2[:, 0:1], in_=sslg[:], axis=AX.X))

    @blk.scalar
    def _(s):
        for sl in range(NSL):
            s.wait_ge(c.ds[0].h, c_ld[sl])
            if sl >= 1:
                s.wait_ge(VE.h, vc(sl - 1, 1))
            _ES(c, s)(s.mul(out=t1g[:], in_=g2sl[sl % 2][:],
                            mul=c.rs2[:, 3:4]))
            s.wait_ge(VE.h, vc(sl, 2))
            _ES(c, s)(s.mul(out=t2g[:], in_=t2g[:], mul=c.rs2[:, 1:2]))
            _ES(c, s)(s.activation(out=t3g[:], in_=t2g[:], func=AF.Sigmoid))
            _ES(c, s)(s.mul(out=t3g[:], in_=t3g[:], mul=c.cc(65536.0)))
        s.wait_ge(TE.h, t_ar3_bc)
        _ES(c, s)(s.copy(out=rs3[:, 0:2], in_=c.pb[6][:, 0:2]))
        for sl in range(NSL):
            s.wait_ge(c.ds[0].h, d_ld0[sl])
            if sl >= 1:
                s.wait_ge(VE.h, vd(sl - 1, 6))
            else:
                s.wait_ge(VE.h, v_ar3b)
            _ES(c, s)(s.mul(out=t1g[:], in_=g2sl[sl % 2][:], mul=rs3[:, 1:2]))
            s.wait_ge(c.ds[1].h, d_ld1[sl])
            _ES(c, s)(s.mul(out=t3g[:], in_=up2[sl % 2][:], mul=c.rs2[:, 2:3]))
            s.wait_ge(VE.h, vd(sl, 2))
            _ES(c, s)(s.mul(out=t1g[:], in_=t2g[:], mul=rs3[:, 0:1]))
        s.wait_ge(TE.h, t_ar4_bc)
        _ES(c, s)(s.copy(out=rs3[:, 2:4], in_=c.pb[6][:, 0:2]))
        for tt in range(NT):
            s.wait_ge((c.ds[0] if tt % 2 == 0 else c.ds[5]).h, f_ld0[tt])
            if tt >= 1:
                s.wait_ge(VE.h, vf(tt - 1, 1))
            _ES(c, s)(s.mul(out=tq1[:],
                            in_=prodt[tt % 2].rearrange("p a b -> p (a b)"),
                            mul=rs3[:, 3:4]))
            s.wait_ge(VE.h, vf(tt, 2))
            if tt >= 2:
                s.wait_ge(TE.h, tf_mm(tt - 2, 48))
            _ES(c, s)(s.mul(out=ot[tt % 2].rearrange("p a b -> p (a b)"),
                            in_=tq2[:], mul=rs3[:, 2:3]))
            s.wait_ge(VE.h, vf(tt, 5))
            if tt >= 2:
                s.wait_ge(c.ds[2 + tt % 2 * 2].h, f_st[tt - 2])
            _ES(c, s)(s.mul(out=mc2[:, 1:2], in_=mc2[:, 0:1],
                            mul=c.cc(1.0 / 1024.0)))
            negmu = _neg2(c, s, mc2, 1)
            _ES(c, s)(s.activation(out=junkg[:], in_=sslg[:], func=AF.Identity,
                                   bias=negmu))
            _ES(c, s)(s.activation(out=sslg[:], in_=junkg[:], func=AF.Square,
                                   accum_out=mc2[:, 2:3]))
            _ES(c, s)(s.mul(out=mc2[:, 2:3], in_=mc2[:, 2:3],
                            mul=c.cc(1.0 / 1024.0)))
            _ES(c, s)(s.activation(out=mc2[:, 2:3], in_=mc2[:, 2:3], func=AF.Ln,
                                   bias=c.cc(LN_EPS)))
            _ES(c, s)(s.activation(out=mc2[:, 3:4], in_=mc2[:, 2:3],
                                   func=AF.Exp, scale=c.cc(-0.5)))
            _ES(c, s)(s.mul(out=x4sl[tt % 2][:], in_=junkg[:],
                            mul=mc2[:, 3:4]))

    @blk.tensor
    def _(t):
        t.wait_ge(VE.h, v_c_end)
        TE.inc(t.transpose(c.pb[6][0:1, 0:128], mxg[:, 1:2], c.ident_f32[:]))
        t.wait_ge(VE.h, v_ar3b)
        TE.inc(t.matmul(c.pb[6][:, 0:2], c.ones_f32[:], mrowg[0:1, 1:3],
                        start=True, stop=True))
        t.wait_ge(SE.h, s_rs3)
        t.wait_ge(VE.h, v_d_end)
        TE.inc(t.transpose(c.pb[6][0:1, 0:128], mxg[:, 2:3], c.ident_f32[:]))
        t.wait_ge(VE.h, v_ar4b)
        TE.inc(t.matmul(c.pb[6][:, 0:2], c.ones_f32[:], mrowg[0:1, 5:7],
                        start=True, stop=True))
        t.wait_ge(c.ds[4].h, wd_at[-1])
        for tt in range(NT):
            t.wait_ge(SE.h, sf(tt, 2))
            if tt >= 2:
                t.wait_ge(VE.h, vf(tt - 2, 4))
            for ck in range(NSL):
                for hh in range(2):
                    TE.inc(t.matmul(c.pb[(tt % 2) * 2 + hh][:],
                                    ot[tt % 2][:, ck, :],
                                    wdT[:, ck, hh * 512:(hh + 1) * 512],
                                    start=(ck == 0), stop=(ck == NSL - 1)))

    @blk.gpsimd
    def _(g):
        g.wait_ge(VE.h, v_ar3a)
        GQ.inc(g.dma_start(out=c.scr["cc_in"], in_=t1g[0:1, 0:128]), 16)
        g.wait_ge(GQ.h, GQ.n)
        c.GE.inc(g.collective_compute(
            "AllReduce", ALU.max, replica_groups=[list(range(c.n_cores))],
            ins=[c.scr["cc_in"].opt()], outs=[c.scr["cc_out"].opt()]))
        g.wait_ge(c.GE.h, c.GE.n)
        GQ.inc(g.dma_start(out=t1g[0:1, 0:128], in_=c.scr["cc_out"]), 16)
        g.wait_ge(GQ.h, GQ.n)
        assert GQ.n == gq_ar3
        g.wait_ge(VE.h, v_ar4a)
        GQ.inc(g.dma_start(out=c.scr["cc_in"], in_=t1g[0:1, 0:128]), 16)
        g.wait_ge(GQ.h, GQ.n)
        c.GE.inc(g.collective_compute(
            "AllReduce", ALU.max, replica_groups=[list(range(c.n_cores))],
            ins=[c.scr["cc_in"].opt()], outs=[c.scr["cc_out"].opt()]))
        g.wait_ge(c.GE.h, c.GE.n)
        GQ.inc(g.dma_start(out=t1g[0:1, 0:128], in_=c.scr["cc_out"]), 16)
        g.wait_ge(GQ.h, GQ.n)
        assert GQ.n == gq_ar4
    _barrier(c)


def _tap_scr(c, name, scr_name, nrows, ncols, dtype):
    if c.tap != name or c.tap_done:
        return
    c.tap_done = True
    c.tapd = c.nc.dram_tensor("tapd", [nrows, ncols], dtype,
                              kind="ExternalOutput").ap()
    want = BARRIER_STEP * (c.phase_idx + 1) - 1
    bb = _carve(c, f"tapb_{name}", (ncols,), dtype, (c.phase_idx_live, PH_OUT))

    def f(g):
        g.wait_ge(c.PH.h, want)
        for r0 in range(0, nrows, 128):
            c.ST.inc(g.dma_start(out=bb[:],
                                 in_=c.scr[scr_name][r0:r0 + 128, :]), 16)
            g.wait_ge(c.ST.h, c.ST.n)
            c.ST.inc(g.dma_start(out=c.tapd[r0:r0 + 128, :], in_=bb[:]), 16)
            g.wait_ge(c.ST.h, c.ST.n)
    c.block.gpsimd(f)


def _in_maps(inputs, consts):
    base = dict(
        W_qkv=np.asarray(inputs["L0_attn_W_qkv"]),
        W_g=np.asarray(inputs["L0_attn_W_g"]),
        W_proj=np.asarray(inputs["L0_attn_W_proj"]),
        decay=np.asarray(inputs["L0_attn_decay"]).reshape(1, 8),
        sw1=np.asarray(inputs["L0_ffn_shared_w1"]),
        sw2=np.asarray(inputs["L0_ffn_shared_w2"]),
        ew1=np.asarray(inputs["L0_ffn_expert_w1"]),
        ew2=np.asarray(inputs["L0_ffn_expert_w2"]),
        router_w=np.asarray(inputs["L0_ffn_router_w"]),
        ln_gamma=np.asarray(inputs["L0_ffn_ln_gamma"]).reshape(1, D),
        ln_beta=np.asarray(inputs["L0_ffn_ln_beta"]).reshape(1, D),
        wq=np.asarray(inputs["L1_attn_wq"]), wk=np.asarray(inputs["L1_attn_wk"]),
        wv=np.asarray(inputs["L1_attn_wv"]), wo=np.asarray(inputs["L1_attn_wo"]),
        bq=np.asarray(inputs["L1_attn_bq"]).reshape(1, D),
        bk=np.asarray(inputs["L1_attn_bk"]).reshape(1, D),
        bv=np.asarray(inputs["L1_attn_bv"]).reshape(1, D),
        bo=np.asarray(inputs["L1_attn_bo"]).reshape(1, D),
        w_gate=np.asarray(inputs["L1_ffn_w_gate"]),
        w_up=np.asarray(inputs["L1_ffn_w_up"]),
        w_down=np.asarray(inputs["L1_ffn_w_down"]),
        **consts,
    )
    x = np.asarray(inputs["L0_attn_hidden_states"])
    maps = [dict(base, x0=np.ascontiguousarray(x[b])) for b in range(x.shape[0])]
    return [{"in_" + k: v for k, v in m.items()} for m in maps]


def kernel(**inputs):
    nc, c = build_program()
    maps = _in_maps(inputs, _host_constants())
    res = run_bass_kernel_spmd(nc, maps, core_ids=list(range(len(maps))))
    return np.stack([np.asarray(r["out"]) for r in res.results], axis=0)
